# version 11
# baseline (speedup 1.0000x reference)
import contextlib
import numpy as np
import concourse.bass as bass
import concourse.mybir as mybir
from concourse.bass_utils import run_bass_kernel_spmd

F32 = mybir.dt.float32
BF16 = mybir.dt.bfloat16
U8 = mybir.dt.uint8
AF = mybir.ActivationFunctionType
OP = mybir.AluOpType

NCORES = 8
D = 2048
NS = 1024
NM = 16384
TPC = 2048
TB = 512
NBLK = 4
FD0 = 512
NR = NM // FD0
TPG = 24
NT = NR * TPG
UW = 256
NU = NM // UW
EPU = 2
GSZ = 8
LAG_NR = 2
LN05 = float(np.log(0.5))

C_M12, C_M4, C_B4, C_B1275, C_EPS, C_LN05, C_ZERO, C_SP = range(8)

DEBUG_DUMPS = False


def _perm(n):
    blk = np.concatenate([np.arange(0, 256, 2), np.arange(1, 256, 2)])
    return (np.arange(n // 256)[:, None] * 256 + blk[None, :]).ravel()


def build_program():
    nc = bass.Bass("TRN2", target_bir_lowering=False)

    xTs = nc.dram_tensor("xTs", [D, TPC], F32, kind="ExternalInput").ap()
    gpT = nc.dram_tensor("gpT", [1024, NM], U8, kind="ExternalInput").ap()
    upT = nc.dram_tensor("upT", [1024, NM], U8, kind="ExternalInput").ap()
    dpT = nc.dram_tensor("dpT", [8192, D], U8, kind="ExternalInput").ap()
    gsT = nc.dram_tensor("gsT", [64, NM], F32, kind="ExternalInput").ap()
    usT = nc.dram_tensor("usT", [64, NM], F32, kind="ExternalInput").ap()
    dsT = nc.dram_tensor("dsT", [512, D], F32, kind="ExternalInput").ap()
    BTp = nc.dram_tensor("BTp", [D, NS], F32, kind="ExternalInput").ap()
    CTp = nc.dram_tensor("CTp", [NS, D], F32, kind="ExternalInput").ap()
    Dp = nc.dram_tensor("Dp", [128, 16], F32, kind="ExternalInput").ap()
    cstI = nc.dram_tensor("cstI", [128, 8], F32, kind="ExternalInput").ap()
    onesI = nc.dram_tensor("onesI", [128, 1], F32, kind="ExternalInput").ap()
    oT = nc.dram_tensor("oT", [D, TPC], F32, kind="ExternalOutput").ap()

    wgS = nc.dram_tensor("wgS", [16, 128, NM], BF16, kind="Internal").ap()
    wuS = nc.dram_tensor("wuS", [16, 128, NM], BF16, kind="Internal").ap()
    wdS = nc.dram_tensor("wdS", [128, 128, D], BF16, kind="Internal").ap()
    statH = nc.dram_tensor("statH", [2, TB], F32, kind="Internal").ap()
    padH = nc.dram_tensor("padH", [1, 64], F32, kind="Internal").ap()

    dbg = {}
    if DEBUG_DUMPS:
        import os
        if os.environ.get("DBG_FULL_SCRATCH"):
            for nm, shp, dt2 in (("dbg_wgS", [16, 128, NM], BF16),
                                 ("dbg_wuS", [16, 128, NM], BF16),
                                 ("dbg_wdS", [128, 128, D], BF16)):
                dbg[nm] = nc.dram_tensor(nm, shp, dt2, kind="ExternalOutput").ap()

    with contextlib.ExitStack() as st_:
        def sbt(name, shape, dt):
            return st_.enter_context(nc.sbuf_tensor(name, shape, dt))

        arU8 = sbt("arU8", [128, 8192], U8)
        arBF = sbt("arBF", [128, 49152], BF16)
        arF32 = sbt("arF32", [128, 11264], F32)
        thA = sbt("thA", [128, 512], F32)
        h_sb = sbt("h_sb", [128, 8192], BF16)
        inter_sb = sbt("inter_sb", [128, 4096], BF16)
        g1_sb = sbt("g1_sb", [128, 3 * TB], BF16)
        stb_sb = sbt("stb_sb", [128, 4096], BF16)
        xbb_sb = sbt("xbb_sb", [128, 1024], BF16)
        sq_sb = sbt("sq_sb", [128, 1024], BF16)
        btsl_sb = sbt("btsl_sb", [128, 4096], BF16)
        ctsl_sb = sbt("ctsl_sb", [128, 2048], BF16)
        cst = sbt("cst", [128, 8], F32)
        dv = sbt("dv", [128, 16], F32)
        ones_f = sbt("ones_f", [128, 1], F32)
        ones_b = sbt("ones_b", [128, 1], BF16)
        u8pad = sbt("u8pad", [1, 64], U8)
        f32pad = sbt("f32pad", [1, 64], F32)
        s_mean = sbt("s_mean", [1, TB], F32)
        s_ex2 = sbt("s_ex2", [1, TB], F32)
        s_var = sbt("s_var", [1, TB], F32)
        s_rstd = s_var

        ps_g = [st_.enter_context(nc.psum_tensor(f"ps_g{i}", [128, TB], F32)) for i in range(3)]
        ps_u = [st_.enter_context(nc.psum_tensor(f"ps_u{i}", [128, TB], F32)) for i in range(2)]
        ps_d = [st_.enter_context(nc.psum_tensor(f"ps_d{i}", [128, TB], F32)) for i in range(2)]

        semnames = ["s_ld", "s_pk", "s_scl", "s_dva", "s_dvb", "s_dvc",
                    "s_aca", "s_poo", "s_wr", "s_x", "s_bc",
                    "s_stat", "s_brd", "s_h", "s_pg", "s_puu", "s_pd",
                    "s_pa", "s_ge", "s_it", "s_ev", "s_ul", "s_st", "s_pre"]
        SEM = {n: st_.enter_context(nc.semaphore(n)) for n in semnames}

        def pkv(i):
            o = (i % 6) * FD0
            return arU8[:, o:o + FD0]

        U8_T8, U8_HI8, U8_M8L, U8_M8H = 3072, 4096, 5120, 6656
        U8_RING = {U8_T8: 2, U8_HI8: 2, U8_M8L: 3, U8_M8H: 3}

        def u8v(base, i):
            o = base + (i % U8_RING[base]) * FD0
            return arU8[:, o:o + FD0]

        (K_MBL, K_MBH, K_SGL, K_SGH, K_R6L, K_R6H) = range(6)
        (K_QL, K_QH, K_VL) = range(6, 9)

        def bfv(kind, i):
            if kind < 6:
                o = (kind * 3 + (i % 3)) * FD0
            else:
                o = (18 + (kind - 6) * 2 + (i % 2)) * FD0
            return arBF[:, o:o + FD0]

        BF_WG, BF_WU, BF_WD = 12288, 20480, 28672

        def wg_u(slot):
            o = BF_WG + (slot % 2) * 4096
            return arBF[:, o:o + 4096].rearrange("p (k n) -> p k n", k=16)

        def wu_u(slot):
            o = BF_WU + (slot % 2) * 4096
            return arBF[:, o:o + 4096].rearrange("p (k n) -> p k n", k=16)

        def wd_e(eslot):
            o = BF_WD + (eslot % 10) * 2048
            return arBF[:, o:o + 2048]

        def wd_pair(ebase):
            o = BF_WD + (ebase % 10) * 2048
            return arBF[:, o:o + 4096].rearrange("p (e h) -> p e h", e=2)

        h_t = h_sb[:].rearrange("p (k t) -> p k t", k=16)
        inter_t = inter_sb[:].rearrange("p (e t) -> p e t", e=8)

        def g1_t(e):
            o = (e % 3) * TB
            return g1_sb[:, o:o + TB]

        stb_t = stb_sb[:].rearrange("p (k t) -> p k t", k=8)

        def xbb_s(k):
            o = (k % 2) * TB
            return xbb_sb[:, o:o + TB]

        def sq_s(k):
            o = (k % 2) * TB
            return sq_sb[:, o:o + TB]

        def btsl(s):
            o = (s % 2) * 2048
            return btsl_sb[:, o:o + 2048].rearrange("p (k n) -> p k n", k=16)

        def ctsl(s):
            o = (s % 2) * 1024
            return ctsl_sb[:, o:o + 1024].rearrange("p (k n) -> p k n", k=8)

        acc_t = arF32[:, 0:8192].rearrange("p (k t) -> p k t", k=16)

        def sclv(i):
            o = 8192 + (i % 4) * FD0
            return arF32[:, o:o + FD0]

        mean_b = arF32[:, 10240:10752]
        rstd_b = arF32[:, 10752:11264]

        def thv(k):
            return thA[:, 0:TB]

        progs = {e: [] for e in ("sync", "gpsimd", "vector", "scalar", "tensor")}
        cnt = {n: 0 for n in semnames}

        def emit(eng, fn):
            progs[eng].append(fn)

        def W(eng, sem, val):
            if val > 0:
                emit(eng, lambda e, s=SEM[sem], v=val: e.wait_ge(s, v))

        def INC(sem, k=1):
            cnt[sem] += k
            return cnt[sem]

        def DMA(eng, out, in_, sem):
            v = INC(sem, 16)
            emit(eng, lambda e, o=out, i=in_, s=SEM[sem]: e.dma_start(out=o, in_=i).then_inc(s, 16))
            return v

        DMA("sync", cst[:], cstI, "s_ld")
        DMA("sync", dv[:], Dp, "s_ld")
        DMA("sync", ones_f[:], onesI, "s_ld")
        W("vector", "s_ld", 48)
        emit("vector", lambda e: e.tensor_copy(ones_b[:], ones_f[:]).then_inc(SEM["s_pre"], 1))
        INC("s_pre")
        W("scalar", "s_ld", 48)
        W("gpsimd", "s_ld", 48)

        tiles = []
        for nr in range(NR):
            for c in range(8):
                tiles.append(("g", gpT, gsT, wgS, c, nr))
            for c in range(8):
                tiles.append(("u", upT, usT, wuS, c, nr))
            for c2 in range(2):
                cd = 2 * nr + c2
                for hr in range(4):
                    tiles.append(("d", dpT, dsT, wdS, cd, hr))
        assert len(tiles) == NT

        def tile_srcs(i):
            kind, psrc, ssrc, dst, c, nr = tiles[i]
            pk_ap = psrc[128 * c:128 * (c + 1), FD0 * nr:FD0 * (nr + 1)]
            scl_ap = (ssrc[8 * c:8 * c + 8, FD0 * nr:FD0 * (nr + 1)]
                      .unsqueeze(1).broadcast_to([8, 16, FD0]))
            return pk_ap, scl_ap

        def tile_dsts(i):
            kind, psrc, ssrc, dst, c, nr = tiles[i]
            if kind in ("g", "u"):
                return (dst[2 * c, :, FD0 * nr:FD0 * (nr + 1)],
                        dst[2 * c + 1, :, FD0 * nr:FD0 * (nr + 1)])
            return (dst[:, 2 * c, FD0 * nr:FD0 * (nr + 1)],
                    dst[:, 2 * c + 1, FD0 * nr:FD0 * (nr + 1)])

        def emit_vecA(i):
            pk = pkv(i)
            W("vector", "s_pk", 16 * min(i + 3, NT + 2))
            emit("vector", lambda e, a=u8v(U8_T8, i), b=pk: e.tensor_scalar(a[:], b[:], 8, None, OP.bitwise_and).then_inc(SEM["s_dva"], 1))
            emit("vector", lambda e, a=u8v(U8_HI8, i), b=pk: e.tensor_scalar(a[:], b[:], 4, None, OP.logical_shift_right))
            emit("vector", lambda e, a=u8v(U8_M8L, i), b=pk: e.tensor_scalar(a[:], b[:], 7, None, OP.bitwise_and).then_inc(SEM["s_dva"], 1))
            emit("vector", lambda e, a=u8v(U8_M8H, i), b=u8v(U8_HI8, i): e.tensor_scalar(a[:], b[:], 7, None, OP.bitwise_and).then_inc(SEM["s_dva"], 1))
            emit("vector", lambda e, a=bfv(K_SGL, i), b=u8v(U8_T8, i): e.tensor_scalar(a[:], b[:], -0.25, 1.0, OP.mult, OP.add))

        def emit_scalA(i):
            pk = pkv(i)
            W("scalar", "s_dva", 3 * (i + 1))
            W("scalar", "s_pk", 16 * min(i + 3, NT + 2))
            if i >= 3:
                W("scalar", "s_dvb", i - 2)
                W("scalar", "s_dvc", i - 2)
                W("scalar", "s_wr", 32 * (i - 2))
            for fn_ in (
                lambda e, a=bfv(K_SGH, i), b=pk: e.activation(a[:], b[:], AF.Sign, scale=-1.0, bias=cst[:, C_B1275:C_B1275 + 1]).then_inc(SEM["s_aca"], 1),
                lambda e, a=bfv(K_R6L, i), b=u8v(U8_M8L, i): e.activation(a[:], b[:], AF.Relu, scale=2.0, bias=cst[:, C_M12:C_M12 + 1]).then_inc(SEM["s_aca"], 1),
                lambda e, a=bfv(K_R6H, i), b=u8v(U8_M8H, i): e.activation(a[:], b[:], AF.Relu, scale=2.0, bias=cst[:, C_M12:C_M12 + 1]).then_inc(SEM["s_aca"], 1),
                lambda e, a=bfv(K_MBL, i), b=u8v(U8_M8L, i): e.activation(a[:], b[:], AF.Identity).then_inc(SEM["s_aca"], 1),
                lambda e, a=bfv(K_MBH, i), b=u8v(U8_M8H, i): e.activation(a[:], b[:], AF.Identity).then_inc(SEM["s_aca"], 1),
            ):
                emit("scalar", fn_)

        def emit_decode_tile(i):
            pk_ap, _ = tile_srcs(i)
            if i >= 6:
                W("sync", "s_aca", 5 * (i - 5))
                W("sync", "s_dva", 3 * (i - 5))
            DMA("sync", pkv(i)[:], pk_ap, "s_pk")

            W("vector", "s_aca", 5 * (i + 1))
            if i >= 2:
                W("vector", "s_poo", i - 1)
                W("vector", "s_wr", 32 * (i - 1))
            emit("vector", lambda e, a=bfv(K_QL, i), b=bfv(K_R6L, i), c2=bfv(K_MBL, i): e.tensor_tensor(a[:], b[:], c2[:], OP.add))
            emit("vector", lambda e, a=bfv(K_QH, i), b=bfv(K_R6H, i), c2=bfv(K_MBH, i): e.tensor_tensor(a[:], b[:], c2[:], OP.add))
            emit("vector", lambda e, a=bfv(K_QL, i): e.tensor_scalar(a[:], a[:], -4.0, 0.0, OP.add, OP.max))
            emit("vector", lambda e, a=bfv(K_QH, i): e.tensor_scalar(a[:], a[:], -4.0, 0.0, OP.add, OP.max))
            emit("vector", lambda e, a=bfv(K_R6L, i), b=bfv(K_MBL, i), c2=bfv(K_QL, i): e.tensor_tensor(a[:], b[:], c2[:], OP.add))
            emit("vector", lambda e, a=bfv(K_R6H, i), b=bfv(K_MBH, i), c2=bfv(K_QH, i): e.tensor_tensor(a[:], b[:], c2[:], OP.add))
            emit("vector", lambda e, a=bfv(K_QH, i), b=bfv(K_R6H, i), c2=bfv(K_SGH, i): e.tensor_tensor(a[:], b[:], c2[:], OP.mult).then_inc(SEM["s_dvc"], 1))
            W("vector", "s_scl", 16 * min(i + 3, NT + 2))
            emit("vector", lambda e, a=bfv(K_VL, i), b=bfv(K_R6L, i), c2=bfv(K_SGL, i): e.tensor_tensor(a[:], b[:], c2[:], OP.mult))
            emit("vector", lambda e, a=bfv(K_VL, i), c2=sclv(i): e.tensor_tensor(a[:], a[:], c2[:], OP.mult).then_inc(SEM["s_dvb"], 1))
            if i + 3 < NT:
                emit_vecA(i + 3)

            emit_scalA(i)
            if i >= 2:
                lo_dst, hi_dst = tile_dsts(i - 2)
                W("scalar", "s_dvb", i - 1)
                W("scalar", "s_poo", i - 1)
                emit("scalar", lambda e, o=lo_dst, s2=bfv(K_VL, i - 2), s=SEM["s_wr"]: e.dma_start(out=o, in_=s2[:]).then_inc(s, 16))
                emit("scalar", lambda e, o=hi_dst, s2=bfv(K_R6H, i - 2), s=SEM["s_wr"]: e.dma_start(out=o, in_=s2[:]).then_inc(s, 16))

            if i + 3 < NT:
                _, scl_ap3 = tile_srcs(i + 3)
                if i >= 1:
                    W("gpsimd", "s_dvb", i)
                DMA("gpsimd", sclv(i + 3)[:], scl_ap3, "s_scl")
            elif i + 3 in (NT, NT + 1):
                DMA("gpsimd", f32pad[:], gsT[0:1, 0:64], "s_scl")
            W("gpsimd", "s_dvc", i + 1)
            W("gpsimd", "s_scl", 16 * min(i + 3, NT + 2))
            if i >= 2:
                W("gpsimd", "s_wr", 32 * (i - 1))
            emit("gpsimd", lambda e, a=bfv(K_R6H, i), b=bfv(K_QH, i), c2=sclv(i): e.tensor_tensor(a[:], b[:], c2[:], OP.mult).then_inc(SEM["s_poo"], 1))

        def emit_decode_prologue():
            for k in range(min(3, NT)):
                _, scl_ap0 = tile_srcs(k)
                DMA("gpsimd", sclv(k)[:], scl_ap0, "s_scl")
            for k in range(min(3, NT)):
                emit_vecA(k)

        def emit_decode_tail():
            for j in (NT - 2, NT - 1):
                lo_dst, hi_dst = tile_dsts(j)
                W("scalar", "s_dvb", j + 1)
                W("scalar", "s_poo", j + 1)
                emit("scalar", lambda e, o=lo_dst, s2=bfv(K_VL, j), s=SEM["s_wr"]: e.dma_start(out=o, in_=s2[:]).then_inc(s, 16))
                emit("scalar", lambda e, o=hi_dst, s2=bfv(K_R6H, j), s=SEM["s_wr"]: e.dma_start(out=o, in_=s2[:]).then_inc(s, 16))
            for dk in range(4):
                emit("scalar", lambda e, k=dk, s=SEM["s_wr"]: e.dma_start(out=padH[0:1, 16 * k:16 * (k + 1)], in_=f32pad[0:1, 0:16]).then_inc(s, 16))
            cnt["s_wr"] = 32 * NT + 64
            DMA("sync", u8pad[:], gpT[0:1, 0:64], "s_pk")
            DMA("sync", u8pad[:], gpT[0:1, 64:128], "s_pk")

        def unit_wr_need(j):
            return min(32 * TPG * (j // 2 + 2), 32 * NT + 64)

        hist = dict(gelu=[], it=[], ugate=[], uup=[], gpd=[], dnev=[],
                    eta=0, unit=0, tanh_last=0, attn_ev=None, hv1=0, hv2=0)

        def layer_norm(fold05):
            ev0 = cnt["s_ev"]
            W("tensor", "s_ev", ev0)
            W("tensor", "s_pre", 1)
            stat0 = cnt["s_stat"]
            pa0 = cnt["s_pa"]
            for kc in range(16):
                if kc >= 2:
                    W("vector", "s_pa", pa0 + kc - 1)
                emit("vector", lambda e, k=kc: e.tensor_copy(xbb_s(k)[:], acc_t[:, k, :]))
                emit("vector", lambda e, k=kc: e.tensor_tensor(sq_s(k)[:], xbb_s(k)[:], xbb_s(k)[:], OP.mult).then_inc(SEM["s_stat"], 1))
                INC("s_stat")
            W("vector", "s_pa", pa0 + 15)
            emit("vector", lambda e: e.tensor_copy(xbb_s(0)[:], acc_t[:, 0, :]).then_inc(SEM["s_stat"], 1))
            INC("s_stat")
            for kc in range(16):
                W("tensor", "s_stat", min(stat0 + kc + 2, stat0 + 17))
                emit("tensor", lambda e, k=kc: e.matmul(ps_d[0][0:1, :], ones_b[:], xbb_s(k)[:], start=(k == 0), stop=(k == 15)))
                if kc < 15:
                    emit("tensor", lambda e, k=kc: e.matmul(ps_d[1][0:1, :], ones_b[:], sq_s(k)[:], start=(k == 0), stop=False).then_inc(SEM["s_pa"], 1))
                else:
                    emit("tensor", lambda e: e.matmul(ps_d[1][0:1, :], ones_b[:], sq_s(15)[:], start=False, stop=True).then_inc(SEM["s_pa"], 1))
                INC("s_pa")
            W("scalar", "s_pa", cnt["s_pa"])
            emit("scalar", lambda e: e.activation(s_mean[:], ps_d[0][0:1, :], AF.Identity, scale=1.0 / D).then_inc(SEM["s_stat"], 1))
            INC("s_stat")
            emit("scalar", lambda e: e.activation(s_ex2[:], ps_d[1][0:1, :], AF.Identity, scale=1.0 / D).then_inc(SEM["s_stat"], 1))
            sv2 = INC("s_stat")
            W("vector", "s_stat", sv2)
            emit("vector", lambda e: e.tensor_tensor(s_var[:], s_mean[:], s_mean[:], OP.mult))
            emit("vector", lambda e: e.tensor_tensor(s_var[:], s_ex2[:], s_var[:], OP.subtract).then_inc(SEM["s_stat"], 1))
            sv3 = INC("s_stat")
            W("scalar", "s_stat", sv3)
            emit("scalar", lambda e: e.activation(s_ex2[:], s_var[:], AF.Ln, bias=cst[0:1, C_EPS:C_EPS + 1]))
            bc = C_LN05 if fold05 else C_ZERO
            emit("scalar", lambda e, b=bc: e.activation(s_rstd[:], s_ex2[:], AF.Exp, scale=-0.5, bias=cst[0:1, b:b + 1]).then_inc(SEM["s_stat"], 1))
            sv4 = INC("s_stat")
            W("sync", "s_stat", sv4)
            DMA("sync", statH[0:1, :], s_mean[:], "s_brd")
            hb = DMA("sync", statH[1:2, :], s_rstd[:], "s_brd")
            W("sync", "s_brd", hb)
            DMA("sync", mean_b[:], statH[0:1, :].unsqueeze(1).broadcast_to([1, 128, TB]), "s_brd")
            DMA("sync", rstd_b[:], statH[1:2, :].unsqueeze(1).broadcast_to([1, 128, TB]), "s_brd")
            W("vector", "s_brd", cnt["s_brd"])
            for kc in range(16):
                emit("vector", lambda e, k=kc: e.tensor_tensor(thv(k)[:], acc_t[:, k, :], mean_b[:], OP.subtract))
                emit("vector", lambda e, k=kc: e.tensor_tensor(h_t[:, k, :], thv(k)[:], rstd_b[:], OP.mult).then_inc(SEM["s_h"], 1))
            return INC("s_h", 16)

        def emit_attn(p):
            pa0 = cnt["s_pa"]
            pd0 = cnt["s_pd"]
            ge0 = cnt["s_ge"]
            bt_done, ct_done = [], []
            for nsc in range(8):
                if nsc >= 2:
                    W("gpsimd", "s_pa", pa0 + nsc - 1)
                elif p >= 1:
                    W("gpsimd", "s_pa", 40 * (p - 1) + 16 + 7 + nsc)
                else:
                    W("gpsimd", "s_pa", pa0)
                bt_done.append(DMA("gpsimd", btsl(nsc)[:, :, :],
                                   BTp.rearrange("(c p) n -> p c n", p=128)[:, :, 128 * nsc:128 * nsc + 128], "s_bc"))
            for dc in range(16):
                if dc >= 2:
                    W("gpsimd", "s_pd", pd0 + dc - 1)
                elif p >= 1:
                    W("gpsimd", "s_pd", 272 * (p - 1) + 15 + dc)
                else:
                    W("gpsimd", "s_pd", pd0)
                ct_done.append(DMA("gpsimd", ctsl(dc)[:, :, :],
                                   CTp.rearrange("(c p) n -> p c n", p=128)[:, :, 128 * dc:128 * dc + 128], "s_bc"))
            all_bc = DMA("gpsimd", f32pad[:], gsT[0:1, 0:64], "s_bc")

            W("tensor", "s_h", hist["hv1"])
            tanh_ge = []
            for nsc in range(8):
                bank = ps_g[nsc % 3]
                W("tensor", "s_bc", bt_done[nsc + 1] if nsc + 1 < 8 else ct_done[0])
                W("tensor", "s_ge", tanh_ge[nsc - 3] if nsc >= 3 else ge0)
                for kc in range(16):
                    if kc < 15:
                        emit("tensor", lambda e, k=kc, n=nsc, b=bank: e.matmul(b[:], btsl(n)[:, k, :], h_t[:, k, :], start=(k == 0), stop=False))
                    else:
                        emit("tensor", lambda e, n=nsc, b=bank: e.matmul(b[:], btsl(n)[:, 15, :], h_t[:, 15, :], start=False, stop=True).then_inc(SEM["s_pa"], 1))
                pav = INC("s_pa")
                W("scalar", "s_pa", pav)
                emit("scalar", lambda e, n=nsc, b=bank: e.activation(stb_t[:, n, :], b[:], AF.Tanh).then_inc(SEM["s_ge"], 1))
                tanh_ge.append(INC("s_ge"))
            hist["tanh_last"] = cnt["s_ge"]
            ev0 = cnt["s_ev"]
            attn_ev = []
            W("tensor", "s_ge", tanh_ge[7])
            for dc in range(16):
                bank = ps_d[dc % 2]
                W("tensor", "s_bc", ct_done[dc + 1] if dc + 1 < 16 else all_bc)
                W("tensor", "s_ev", attn_ev[dc - 2] if dc >= 2 else ev0)
                for nsc in range(8):
                    if nsc < 7:
                        emit("tensor", lambda e, n=nsc, d2=dc, b=bank: e.matmul(b[:], ctsl(d2)[:, n, :], stb_t[:, n, :], start=(n == 0), stop=False))
                    else:
                        emit("tensor", lambda e, d2=dc, b=bank: e.matmul(b[:], ctsl(d2)[:, 7, :], stb_t[:, 7, :], start=False, stop=True).then_inc(SEM["s_pd"], 1))
                pdv = INC("s_pd")
                W("vector", "s_pd", pdv)
                emit("vector", lambda e, d2=dc, b=bank: e.tensor_tensor(acc_t[:, d2, :], acc_t[:, d2, :], b[:], OP.add))
                emit("vector", lambda e, d2=dc: e.scalar_tensor_tensor(acc_t[:, d2, :], h_t[:, d2, :], dv[:, d2:d2 + 1], acc_t[:, d2, :], OP.mult, OP.add).then_inc(SEM["s_ev"], 1))
                attn_ev.append(INC("s_ev"))
            hist["attn_ev"] = attn_ev

        def emit_pass_prologue(p):
            tsl = slice(TB * p, TB * (p + 1))
            W("sync", "s_st", cnt["s_st"])
            xv = DMA("sync", acc_t[:, :, :], xTs.rearrange("(c p) t -> p c t", p=128)[:, :, tsl], "s_x")
            W("vector", "s_x", xv)
            W("vector", "s_pd", cnt["s_pd"])
            W("vector", "s_pg", cnt["s_pg"])
            W("vector", "s_puu", cnt["s_puu"])
            hist["hv1"] = layer_norm(False)
            emit_attn(p)
            W("vector", "s_ev", hist["attn_ev"][15])
            hist["hv2"] = layer_norm(True)
            W("tensor", "s_h", hist["hv2"])

        def emit_unit_loads(p, j, chase, wd_only=False, skip_wd=False):
            uidx = p * NU + j
            tsl_n = slice(UW * j, UW * (j + 1))
            wslot = j % 2
            qg, qu, qd = ("sync", "sync", "sync") if p == 0 else ("sync", "scalar", "gpsimd")
            if not wd_only:
                if uidx >= 2:
                    W(qg, "s_pg", hist["ugate"][uidx - 2])
                    W(qu, "s_puu", hist["uup"][uidx - 2])
                if chase:
                    W("sync", "s_wr", unit_wr_need(j))
                    if p > 0:
                        W(qu, "s_wr", unit_wr_need(j))
                DMA(qg, wg_u(wslot)[:, :, :], wgS[:, :, tsl_n].rearrange("k p n -> p k n"), "s_ul")
                DMA(qu, wu_u(wslot)[:, :, :], wuS[:, :, tsl_n].rearrange("k p n -> p k n"), "s_ul")
            if skip_wd:
                return
            e2 = 2 * j
            if e2 >= 10:
                W(qd, "s_pd", hist["gpd"][16 * p + (e2 - 10) // GSZ])
            elif p >= 1:
                g = 15 if e2 < 8 else 14
                W(qd, "s_pd", 272 * (p - 1) + 16 + 16 * (g + 1))
            if chase and p > 0:
                W(qd, "s_wr", unit_wr_need(j))
            DMA(qd, wd_pair(2 * j)[:, :, :], wdS[:, 2 * j:2 * j + 2, :], "s_ul")

        def emit_ffn_unit(p, j, chase=False, loads=True):
            uidx = hist["unit"]
            wslot = j % 2
            if loads:
                emit_unit_loads(p, j, chase)
            if p == 0:
                W("tensor", "s_ul", 48 * (uidx + 1))
            elif j == NU - 1 and p < NBLK - 1:
                W("tensor", "s_ul", 48 * (uidx + 1) + 32)
            else:
                W("tensor", "s_ul", min(48 * (uidx + 2), 48 * 4 * NU + 48))
            for ep in range(EPU):
                E = hist["eta"]
                eloc = E % 128
                gb = ps_g[E % 3]
                ub = ps_u[E % 2]
                geg = max(hist["gelu"][E - 3] if E >= 3 else 0, hist["tanh_last"])
                W("tensor", "s_ge", geg)
                for kk in range(16):
                    if kk < 15:
                        emit("tensor", lambda e, k=kk, pp=ep, b=gb, ws=wslot: e.matmul(b[:], wg_u(ws)[:, k, 128 * pp:128 * pp + 128], h_t[:, k, :], start=(k == 0), stop=False))
                    else:
                        emit("tensor", lambda e, pp=ep, b=gb, ws=wslot: e.matmul(b[:], wg_u(ws)[:, 15, 128 * pp:128 * pp + 128], h_t[:, 15, :], start=False, stop=True).then_inc(SEM["s_pg"], 1))
                pgv = INC("s_pg")
                W("tensor", "s_it", hist["it"][E - 2] if E >= 2 else 0)
                for kk in range(16):
                    if kk < 15:
                        emit("tensor", lambda e, k=kk, pp=ep, b=ub, ws=wslot: e.matmul(b[:], wu_u(ws)[:, k, 128 * pp:128 * pp + 128], h_t[:, k, :], start=(k == 0), stop=False))
                    else:
                        emit("tensor", lambda e, pp=ep, b=ub, ws=wslot: e.matmul(b[:], wu_u(ws)[:, 15, 128 * pp:128 * pp + 128], h_t[:, 15, :], start=False, stop=True).then_inc(SEM["s_puu"], 1))
                puv = INC("s_puu")
                W("scalar", "s_pg", pgv)
                W("scalar", "s_it", hist["it"][E - 3] if E >= 3 else 0)
                emit("scalar", lambda e, b=gb, g=g1_t(E): e.activation(g[:], b[:], AF.Gelu_apprx_tanh).then_inc(SEM["s_ge"], 1))
                hist["gelu"].append(INC("s_ge"))
                W("vector", "s_ge", hist["gelu"][E])
                W("vector", "s_puu", puv)
                if eloc >= GSZ:
                    W("vector", "s_pd", hist["gpd"][-1])
                elif p > 0 and hist["gpd"]:
                    W("vector", "s_pd", hist["gpd"][-1])
                emit("vector", lambda e, b=ub, g=g1_t(E), s2=eloc % GSZ: e.scalar_tensor_tensor(inter_t[:, s2, :], g[:], 0.5, b[:], OP.mult, OP.mult).then_inc(SEM["s_it"], 1))
                hist["it"].append(INC("s_it"))
                hist["eta"] += 1
            hist["ugate"].append(cnt["s_pg"])
            hist["uup"].append(cnt["s_puu"])
            hist["unit"] = uidx + 1

            if j % 4 == 3:
                W("tensor", "s_it", hist["it"][hist["eta"] - 1])
                for hc in range(16):
                    db = ps_d[hc % 2]
                    li = len(hist["dnev"])
                    evg = hist["attn_ev"][14 + (hc % 2)]
                    if li >= 2:
                        evg = max(evg, hist["dnev"][li - 2])
                    W("tensor", "s_ev", evg)
                    for ei in range(GSZ):
                        egl = (hist["eta"] - GSZ) + ei
                        eloc2 = egl % 128
                        esl = (2 * (eloc2 // 2)) % 10 + (eloc2 % 2)
                        if ei < GSZ - 1:
                            emit("tensor", lambda e, s3=esl, h2=hc, s2=ei, b=db: e.matmul(b[:], wd_e(s3)[:, 128 * h2:128 * h2 + 128], inter_t[:, s2, :], start=(s2 == 0), stop=False))
                        else:
                            emit("tensor", lambda e, s3=esl, h2=hc, s2=ei, b=db: e.matmul(b[:], wd_e(s3)[:, 128 * h2:128 * h2 + 128], inter_t[:, s2, :], start=False, stop=True).then_inc(SEM["s_pd"], 1))
                    pdv = INC("s_pd")
                    W("vector", "s_pd", pdv)
                    emit("vector", lambda e, h2=hc, b=db: e.tensor_tensor(acc_t[:, h2, :], acc_t[:, h2, :], b[:], OP.add).then_inc(SEM["s_ev"], 1))
                    hist["dnev"].append(INC("s_ev"))
                hist["gpd"].append(cnt["s_pd"])

        def emit_store(p):
            tsl = slice(TB * p, TB * (p + 1))
            W("scalar", "s_ev", hist["dnev"][-1])
            DMA("scalar", oT.rearrange("(c p) t -> p c t", p=128)[:, :, tsl], acc_t[:, :, :], "s_st")

        emit_pass_prologue(0)
        emit_decode_prologue()
        ti = 0
        uj = 0
        for nr in range(NR):
            for _ in range(TPG):
                emit_decode_tile(ti)
                ti += 1
            if nr >= LAG_NR:
                tgt = 2 * (nr - LAG_NR) + 2
                while uj < min(tgt, NU):
                    emit_ffn_unit(0, uj, chase=True)
                    uj += 1
        emit_decode_tail()
        while uj < NU:
            emit_ffn_unit(0, uj, chase=True)
            uj += 1
        emit_store(0)
        if DEBUG_DUMPS:
            for nm2 in ("dbg_wgS", "dbg_wuS", "dbg_wdS"):
                if nm2 in dbg:
                    src2 = {"dbg_wgS": wgS, "dbg_wuS": wuS, "dbg_wdS": wdS}[nm2]
                    W("gpsimd", "s_wr", 32 * NT)
                    emit("gpsimd", lambda e, o=dbg[nm2], i=src2, s=SEM["s_st"]: e.dma_start(out=o, in_=i).then_inc(s, 16))
                    INC("s_st", 16)

        for p in range(1, NBLK):
            emit_pass_prologue(p)
            if p == 1:
                emit_unit_loads(1, 0, chase=True)
            else:
                emit_unit_loads(p, 0, chase=False, wd_only=True)
            for j in range(NU):
                if j + 1 < NU:
                    emit_unit_loads(p, j + 1, chase=(p == 1))
                elif p + 1 < NBLK:
                    emit_unit_loads(p + 1, 0, chase=False, skip_wd=True)
                else:
                    DMA("sync", u8pad[:], gpT[0:1, 0:64], "s_ul")
                    DMA("scalar", f32pad[0:1, 0:16], gsT[0:1, 0:16], "s_ul")
                    DMA("gpsimd", f32pad[0:1, 16:32], gsT[0:1, 16:32], "s_ul")
                emit_ffn_unit(p, j, loads=False)
            emit_store(p)

        W("gpsimd", "s_st", cnt["s_st"])
        W("gpsimd", "s_wr", 32 * NT + 64)

        with nc.Block() as block:
            @block.sync
            def _(eng):
                for fn in progs["sync"]:
                    fn(eng)

            @block.gpsimd
            def _(eng):
                for fn in progs["gpsimd"]:
                    fn(eng)

            @block.vector
            def _(eng):
                for fn in progs["vector"]:
                    fn(eng)

            @block.scalar
            def _(eng):
                for fn in progs["scalar"]:
                    fn(eng)

            @block.tensor
            def _(eng):
                for fn in progs["tensor"]:
                    fn(eng)

    return nc


def kernel(L0_attn_x, L0_attn_B, L0_attn_C, L0_attn_D,
           L0_ffn_gate_W_packed, L0_ffn_gate_scales,
           L0_ffn_up_W_packed, L0_ffn_up_scales,
           L0_ffn_down_W_packed, L0_ffn_down_scales):
    pd_ = _perm(D)
    pn = _perm(NM)

    x = np.ascontiguousarray(np.asarray(L0_attn_x, np.float32).reshape(-1, D).T[pd_])
    BTp = np.ascontiguousarray(np.asarray(L0_attn_B, np.float32).T[pd_])
    CTp = np.ascontiguousarray(np.asarray(L0_attn_C, np.float32).T[:, pd_])
    Dp = np.ascontiguousarray(np.asarray(L0_attn_D, np.float32)[pd_].reshape(16, 128).T)

    def pack_u8(Wp):
        w = np.ascontiguousarray(np.asarray(Wp))
        return w.view(np.uint8).reshape(w.shape[0], w.shape[1], 4)[:, :, 0]

    gpTh = np.ascontiguousarray(pack_u8(L0_ffn_gate_W_packed).T[:, pn])
    upTh = np.ascontiguousarray(pack_u8(L0_ffn_up_W_packed).T[:, pn])
    dpTh = np.ascontiguousarray(pack_u8(L0_ffn_down_W_packed).T[:, pd_])
    gsTh = np.ascontiguousarray(np.asarray(L0_ffn_gate_scales, np.float32).T[:, pn])
    usTh = np.ascontiguousarray(np.asarray(L0_ffn_up_scales, np.float32).T[:, pn])
    dsTh = np.ascontiguousarray(np.asarray(L0_ffn_down_scales, np.float32).T[:, pd_])

    cstv = np.zeros((128, 8), np.float32)
    cstv[:, C_M12] = -12.0
    cstv[:, C_M4] = -4.0
    cstv[:, C_B4] = 4.0
    cstv[:, C_B1275] = 127.5
    cstv[:, C_EPS] = 1e-5
    cstv[:, C_LN05] = LN05
    ones = np.ones((128, 1), np.float32)

    nc = build_program()
    in_maps = []
    for r in range(NCORES):
        in_maps.append(dict(
            xTs=np.ascontiguousarray(x[:, TPC * r:TPC * (r + 1)]),
            gpT=gpTh, upT=upTh, dpT=dpTh, gsT=gsTh, usT=usTh, dsT=dsTh,
            BTp=BTp, CTp=CTp, Dp=Dp, cstI=cstv, onesI=ones,
        ))
    res = run_bass_kernel_spmd(nc, in_maps, core_ids=list(range(NCORES)))
    outT = np.concatenate([r["oT"] for r in res.results], axis=1)
    inv = np.argsort(pd_)
    out = outT[inv].T.reshape(2, 8192, D)
    return np.ascontiguousarray(out, dtype=np.float32)


# version 13
# speedup vs baseline: 1.0895x; 1.0895x over previous
import contextlib
import numpy as np
import concourse.bass as bass
import concourse.mybir as mybir
from concourse.bass_utils import run_bass_kernel_spmd

F32 = mybir.dt.float32
BF16 = mybir.dt.bfloat16
U8 = mybir.dt.uint8
AF = mybir.ActivationFunctionType
OP = mybir.AluOpType

NCORES = 8
D = 2048
NS = 1024
NM = 16384
TPC = 2048
TB = 512
NBLK = 4
FD0 = 512
NR = NM // FD0
TPG = 24
NT = NR * TPG
UW = 256
NU = NM // UW
EPU = 2
GSZ = 8
LAG_NR = 2
LN05 = float(np.log(0.5))

C_M12, C_M4, C_B4, C_B1275, C_EPS, C_LN05, C_ZERO, C_SP = range(8)

DEBUG_DUMPS = False


def _perm(n):
    blk = np.concatenate([np.arange(0, 256, 2), np.arange(1, 256, 2)])
    return (np.arange(n // 256)[:, None] * 256 + blk[None, :]).ravel()


def build_program():
    nc = bass.Bass("TRN2", target_bir_lowering=False)

    xTs = nc.dram_tensor("xTs", [D, TPC], F32, kind="ExternalInput").ap()
    gpT = nc.dram_tensor("gpT", [1024, NM], U8, kind="ExternalInput").ap()
    upT = nc.dram_tensor("upT", [1024, NM], U8, kind="ExternalInput").ap()
    dpT = nc.dram_tensor("dpT", [8192, D], U8, kind="ExternalInput").ap()
    gsT = nc.dram_tensor("gsT", [64, NM], F32, kind="ExternalInput").ap()
    usT = nc.dram_tensor("usT", [64, NM], F32, kind="ExternalInput").ap()
    dsT = nc.dram_tensor("dsT", [512, D], F32, kind="ExternalInput").ap()
    BTp = nc.dram_tensor("BTp", [D, NS], F32, kind="ExternalInput").ap()
    CTp = nc.dram_tensor("CTp", [NS, D], F32, kind="ExternalInput").ap()
    Dp = nc.dram_tensor("Dp", [128, 16], F32, kind="ExternalInput").ap()
    cstI = nc.dram_tensor("cstI", [128, 8], F32, kind="ExternalInput").ap()
    onesI = nc.dram_tensor("onesI", [128, 1], F32, kind="ExternalInput").ap()
    oT = nc.dram_tensor("oT", [D, TPC], F32, kind="ExternalOutput").ap()

    wgS = nc.dram_tensor("wgS", [16, 128, NM], BF16, kind="Internal").ap()
    wuS = nc.dram_tensor("wuS", [16, 128, NM], BF16, kind="Internal").ap()
    wdS = nc.dram_tensor("wdS", [128, 128, D], BF16, kind="Internal").ap()
    statH = nc.dram_tensor("statH", [2, TB], F32, kind="Internal").ap()
    padH = nc.dram_tensor("padH", [1, 64], F32, kind="Internal").ap()

    dbg = {}
    if DEBUG_DUMPS:
        import os
        if os.environ.get("DBG_FULL_SCRATCH"):
            for nm, shp, dt2 in (("dbg_wgS", [16, 128, NM], BF16),
                                 ("dbg_wuS", [16, 128, NM], BF16),
                                 ("dbg_wdS", [128, 128, D], BF16)):
                dbg[nm] = nc.dram_tensor(nm, shp, dt2, kind="ExternalOutput").ap()

    with contextlib.ExitStack() as st_:
        def sbt(name, shape, dt):
            return st_.enter_context(nc.sbuf_tensor(name, shape, dt))

        arU8 = sbt("arU8", [128, 8192], U8)
        arBF = sbt("arBF", [128, 49152], BF16)
        arF32 = sbt("arF32", [128, 11264], F32)
        thA = sbt("thA", [128, 512], F32)
        h_sb = sbt("h_sb", [128, 8192], BF16)
        inter_sb = sbt("inter_sb", [128, 4096], BF16)
        g1_sb = sbt("g1_sb", [128, 3 * TB], BF16)
        stb_sb = sbt("stb_sb", [128, 4096], BF16)
        xbb_sb = sbt("xbb_sb", [128, 1024], BF16)
        sq_sb = sbt("sq_sb", [128, 1024], BF16)
        btsl_sb = sbt("btsl_sb", [128, 4096], BF16)
        ctsl_sb = sbt("ctsl_sb", [128, 2048], BF16)
        cst = sbt("cst", [128, 8], F32)
        dv = sbt("dv", [128, 16], F32)
        ones_f = sbt("ones_f", [128, 1], F32)
        ones_b = sbt("ones_b", [128, 1], BF16)
        u8pad = sbt("u8pad", [1, 64], U8)
        f32pad = sbt("f32pad", [1, 64], F32)
        s_mean = sbt("s_mean", [1, TB], F32)
        s_ex2 = sbt("s_ex2", [1, TB], F32)
        s_var = sbt("s_var", [1, TB], F32)
        s_rstd = s_var

        ps_g = [st_.enter_context(nc.psum_tensor(f"ps_g{i}", [128, TB], F32)) for i in range(3)]
        ps_u = [st_.enter_context(nc.psum_tensor(f"ps_u{i}", [128, TB], F32)) for i in range(3)]
        ps_d = [st_.enter_context(nc.psum_tensor(f"ps_d{i}", [128, TB], F32)) for i in range(2)]

        semnames = ["s_ld", "s_pk", "s_scl", "s_dva", "s_dvb", "s_dvc",
                    "s_aca", "s_poo", "s_wr", "s_x", "s_bc",
                    "s_stat", "s_brd", "s_h", "s_pg", "s_puu", "s_pd",
                    "s_pa", "s_ge", "s_it", "s_ev", "s_ul", "s_st", "s_pre"]
        SEM = {n: st_.enter_context(nc.semaphore(n)) for n in semnames}

        def pkv(i):
            o = (i % 6) * FD0
            return arU8[:, o:o + FD0]

        U8_T8, U8_HI8, U8_M8L, U8_M8H = 3072, 4096, 5120, 6656
        U8_RING = {U8_T8: 2, U8_HI8: 2, U8_M8L: 3, U8_M8H: 3}

        def u8v(base, i):
            o = base + (i % U8_RING[base]) * FD0
            return arU8[:, o:o + FD0]

        (K_MBL, K_MBH, K_SGL, K_SGH, K_R6L, K_R6H) = range(6)
        (K_QL, K_QH, K_VL) = range(6, 9)

        def bfv(kind, i):
            if kind < 6:
                o = (kind * 3 + (i % 3)) * FD0
            else:
                o = (18 + (kind - 6) * 2 + (i % 2)) * FD0
            return arBF[:, o:o + FD0]

        BF_WG, BF_WU, BF_WD = 12288, 20480, 28672

        def wg_u(slot):
            o = BF_WG + (slot % 2) * 4096
            return arBF[:, o:o + 4096].rearrange("p (k n) -> p k n", k=16)

        def wu_u(slot):
            o = BF_WU + (slot % 2) * 4096
            return arBF[:, o:o + 4096].rearrange("p (k n) -> p k n", k=16)

        def wd_e(eslot):
            o = BF_WD + (eslot % 10) * 2048
            return arBF[:, o:o + 2048]

        def wd_pair(ebase):
            o = BF_WD + (ebase % 10) * 2048
            return arBF[:, o:o + 4096].rearrange("p (e h) -> p e h", e=2)

        h_t = h_sb[:].rearrange("p (k t) -> p k t", k=16)
        inter_t = inter_sb[:].rearrange("p (e t) -> p e t", e=8)

        def g1_t(e):
            o = (e % 3) * TB
            return g1_sb[:, o:o + TB]

        stb_t = stb_sb[:].rearrange("p (k t) -> p k t", k=8)

        def xbb_s(k):
            o = (k % 2) * TB
            return xbb_sb[:, o:o + TB]

        def sq_s(k):
            o = (k % 2) * TB
            return sq_sb[:, o:o + TB]

        def btsl(s):
            o = (s % 2) * 2048
            return btsl_sb[:, o:o + 2048].rearrange("p (k n) -> p k n", k=16)

        def ctsl(s):
            o = (s % 2) * 1024
            return ctsl_sb[:, o:o + 1024].rearrange("p (k n) -> p k n", k=8)

        acc_t = arF32[:, 0:8192].rearrange("p (k t) -> p k t", k=16)

        def sclv(i):
            o = 8192 + (i % 4) * FD0
            return arF32[:, o:o + FD0]

        mean_b = arF32[:, 10240:10752]
        rstd_b = arF32[:, 10752:11264]

        def thv(k):
            return thA[:, 0:TB]

        progs = {e: [] for e in ("sync", "gpsimd", "vector", "scalar", "tensor")}
        cnt = {n: 0 for n in semnames}

        def emit(eng, fn):
            progs[eng].append(fn)

        def W(eng, sem, val):
            if val > 0:
                emit(eng, lambda e, s=SEM[sem], v=val: e.wait_ge(s, v))

        def INC(sem, k=1):
            cnt[sem] += k
            return cnt[sem]

        def DMA(eng, out, in_, sem):
            v = INC(sem, 16)
            emit(eng, lambda e, o=out, i=in_, s=SEM[sem]: e.dma_start(out=o, in_=i).then_inc(s, 16))
            return v

        DMA("sync", cst[:], cstI, "s_ld")
        DMA("sync", dv[:], Dp, "s_ld")
        DMA("sync", ones_f[:], onesI, "s_ld")
        W("vector", "s_ld", 48)
        emit("vector", lambda e: e.tensor_copy(ones_b[:], ones_f[:]).then_inc(SEM["s_pre"], 1))
        INC("s_pre")
        W("scalar", "s_ld", 48)
        W("gpsimd", "s_ld", 48)

        tiles = []
        for nr in range(NR):
            for c in range(8):
                tiles.append(("g", gpT, gsT, wgS, c, nr))
            for c in range(8):
                tiles.append(("u", upT, usT, wuS, c, nr))
            for c2 in range(2):
                cd = 2 * nr + c2
                for hr in range(4):
                    tiles.append(("d", dpT, dsT, wdS, cd, hr))
        assert len(tiles) == NT

        def tile_srcs(i):
            kind, psrc, ssrc, dst, c, nr = tiles[i]
            pk_ap = psrc[128 * c:128 * (c + 1), FD0 * nr:FD0 * (nr + 1)]
            scl_ap = (ssrc[8 * c:8 * c + 8, FD0 * nr:FD0 * (nr + 1)]
                      .unsqueeze(1).broadcast_to([8, 16, FD0]))
            return pk_ap, scl_ap

        def tile_dsts(i):
            kind, psrc, ssrc, dst, c, nr = tiles[i]
            if kind in ("g", "u"):
                return (dst[2 * c, :, FD0 * nr:FD0 * (nr + 1)],
                        dst[2 * c + 1, :, FD0 * nr:FD0 * (nr + 1)])
            return (dst[:, 2 * c, FD0 * nr:FD0 * (nr + 1)],
                    dst[:, 2 * c + 1, FD0 * nr:FD0 * (nr + 1)])

        def emit_vecA(i):
            pk = pkv(i)
            W("vector", "s_pk", 16 * min(i + 3, NT + 2))
            emit("vector", lambda e, a=u8v(U8_T8, i), b=pk: e.tensor_scalar(a[:], b[:], 8, None, OP.bitwise_and).then_inc(SEM["s_dva"], 1))
            emit("vector", lambda e, a=u8v(U8_HI8, i), b=pk: e.tensor_scalar(a[:], b[:], 4, None, OP.logical_shift_right))
            emit("vector", lambda e, a=u8v(U8_M8L, i), b=pk: e.tensor_scalar(a[:], b[:], 7, None, OP.bitwise_and).then_inc(SEM["s_dva"], 1))
            emit("vector", lambda e, a=u8v(U8_M8H, i), b=u8v(U8_HI8, i): e.tensor_scalar(a[:], b[:], 7, None, OP.bitwise_and).then_inc(SEM["s_dva"], 1))
            emit("vector", lambda e, a=bfv(K_SGL, i), b=u8v(U8_T8, i): e.tensor_scalar(a[:], b[:], -0.25, 1.0, OP.mult, OP.add))

        def emit_scalA(i):
            pk = pkv(i)
            W("scalar", "s_dva", 3 * (i + 1))
            W("scalar", "s_pk", 16 * min(i + 3, NT + 2))
            if i >= 3:
                W("scalar", "s_dvb", i - 2)
                W("scalar", "s_dvc", i - 2)
                W("scalar", "s_wr", 32 * (i - 2))
            for fn_ in (
                lambda e, a=bfv(K_SGH, i), b=pk: e.activation(a[:], b[:], AF.Sign, scale=-1.0, bias=cst[:, C_B1275:C_B1275 + 1]).then_inc(SEM["s_aca"], 1),
                lambda e, a=bfv(K_R6L, i), b=u8v(U8_M8L, i): e.activation(a[:], b[:], AF.Relu, scale=2.0, bias=cst[:, C_M12:C_M12 + 1]).then_inc(SEM["s_aca"], 1),
                lambda e, a=bfv(K_R6H, i), b=u8v(U8_M8H, i): e.activation(a[:], b[:], AF.Relu, scale=2.0, bias=cst[:, C_M12:C_M12 + 1]).then_inc(SEM["s_aca"], 1),
                lambda e, a=bfv(K_MBL, i), b=u8v(U8_M8L, i): e.activation(a[:], b[:], AF.Identity).then_inc(SEM["s_aca"], 1),
                lambda e, a=bfv(K_MBH, i), b=u8v(U8_M8H, i): e.activation(a[:], b[:], AF.Identity).then_inc(SEM["s_aca"], 1),
            ):
                emit("scalar", fn_)

        def emit_decode_tile(i):
            pk_ap, _ = tile_srcs(i)
            if i >= 6:
                W("sync", "s_aca", 5 * (i - 5))
                W("sync", "s_dva", 3 * (i - 5))
            DMA("sync", pkv(i)[:], pk_ap, "s_pk")

            W("vector", "s_aca", 5 * (i + 1))
            if i >= 2:
                W("vector", "s_poo", i - 1)
                W("vector", "s_wr", 32 * (i - 1))
            emit("vector", lambda e, a=bfv(K_QL, i), b=bfv(K_R6L, i), c2=bfv(K_MBL, i): e.tensor_tensor(a[:], b[:], c2[:], OP.add))
            emit("vector", lambda e, a=bfv(K_QH, i), b=bfv(K_R6H, i), c2=bfv(K_MBH, i): e.tensor_tensor(a[:], b[:], c2[:], OP.add))
            emit("vector", lambda e, a=bfv(K_QL, i): e.tensor_scalar(a[:], a[:], -4.0, 0.0, OP.add, OP.max))
            emit("vector", lambda e, a=bfv(K_QH, i): e.tensor_scalar(a[:], a[:], -4.0, 0.0, OP.add, OP.max))
            emit("vector", lambda e, a=bfv(K_R6L, i), b=bfv(K_MBL, i), c2=bfv(K_QL, i): e.tensor_tensor(a[:], b[:], c2[:], OP.add))
            emit("vector", lambda e, a=bfv(K_R6H, i), b=bfv(K_MBH, i), c2=bfv(K_QH, i): e.tensor_tensor(a[:], b[:], c2[:], OP.add))
            emit("vector", lambda e, a=bfv(K_QH, i), b=bfv(K_R6H, i), c2=bfv(K_SGH, i): e.tensor_tensor(a[:], b[:], c2[:], OP.mult).then_inc(SEM["s_dvc"], 1))
            W("vector", "s_scl", 16 * min(i + 3, NT + 2))
            emit("vector", lambda e, a=bfv(K_VL, i), b=bfv(K_R6L, i), c2=bfv(K_SGL, i): e.tensor_tensor(a[:], b[:], c2[:], OP.mult))
            emit("vector", lambda e, a=bfv(K_VL, i), c2=sclv(i): e.tensor_tensor(a[:], a[:], c2[:], OP.mult).then_inc(SEM["s_dvb"], 1))
            if i + 3 < NT:
                emit_vecA(i + 3)

            emit_scalA(i)
            if i >= 2:
                lo_dst, hi_dst = tile_dsts(i - 2)
                W("scalar", "s_dvb", i - 1)
                W("scalar", "s_poo", i - 1)
                emit("scalar", lambda e, o=lo_dst, s2=bfv(K_VL, i - 2), s=SEM["s_wr"]: e.dma_start(out=o, in_=s2[:]).then_inc(s, 16))
                emit("scalar", lambda e, o=hi_dst, s2=bfv(K_R6H, i - 2), s=SEM["s_wr"]: e.dma_start(out=o, in_=s2[:]).then_inc(s, 16))

            if i + 3 < NT:
                _, scl_ap3 = tile_srcs(i + 3)
                if i >= 1:
                    W("gpsimd", "s_dvb", i)
                DMA("gpsimd", sclv(i + 3)[:], scl_ap3, "s_scl")
            elif i + 3 in (NT, NT + 1):
                DMA("gpsimd", f32pad[:], gsT[0:1, 0:64], "s_scl")
            W("gpsimd", "s_dvc", i + 1)
            W("gpsimd", "s_scl", 16 * min(i + 3, NT + 2))
            if i >= 2:
                W("gpsimd", "s_wr", 32 * (i - 1))
            emit("gpsimd", lambda e, a=bfv(K_R6H, i), b=bfv(K_QH, i), c2=sclv(i): e.tensor_tensor(a[:], b[:], c2[:], OP.mult).then_inc(SEM["s_poo"], 1))

        def emit_decode_prologue():
            for k in range(min(3, NT)):
                _, scl_ap0 = tile_srcs(k)
                DMA("gpsimd", sclv(k)[:], scl_ap0, "s_scl")
            for k in range(min(3, NT)):
                emit_vecA(k)

        def emit_decode_tail():
            for j in (NT - 2, NT - 1):
                lo_dst, hi_dst = tile_dsts(j)
                W("scalar", "s_dvb", j + 1)
                W("scalar", "s_poo", j + 1)
                emit("scalar", lambda e, o=lo_dst, s2=bfv(K_VL, j), s=SEM["s_wr"]: e.dma_start(out=o, in_=s2[:]).then_inc(s, 16))
                emit("scalar", lambda e, o=hi_dst, s2=bfv(K_R6H, j), s=SEM["s_wr"]: e.dma_start(out=o, in_=s2[:]).then_inc(s, 16))
            for dk in range(4):
                emit("scalar", lambda e, k=dk, s=SEM["s_wr"]: e.dma_start(out=padH[0:1, 16 * k:16 * (k + 1)], in_=f32pad[0:1, 0:16]).then_inc(s, 16))
            cnt["s_wr"] = 32 * NT + 64
            DMA("sync", u8pad[:], gpT[0:1, 0:64], "s_pk")
            DMA("sync", u8pad[:], gpT[0:1, 64:128], "s_pk")

        def unit_wr_need(j):
            return min(32 * TPG * (j // 2 + 2), 32 * NT + 64)

        hist = dict(gelu=[], it=[], ugate=[], uup=[], gpd=[], dnev=[],
                    eta=0, unit=0, tanh_last=0, attn_ev=None, hv1=0, hv2=0)

        def layer_norm(fold05):
            ev0 = cnt["s_ev"]
            W("tensor", "s_ev", ev0)
            W("tensor", "s_pre", 1)
            stat0 = cnt["s_stat"]
            pa0 = cnt["s_pa"]
            for kc in range(16):
                if kc >= 2:
                    W("vector", "s_pa", pa0 + kc - 1)
                emit("vector", lambda e, k=kc: e.tensor_copy(xbb_s(k)[:], acc_t[:, k, :]))
                emit("vector", lambda e, k=kc: e.tensor_tensor(sq_s(k)[:], xbb_s(k)[:], xbb_s(k)[:], OP.mult).then_inc(SEM["s_stat"], 1))
                INC("s_stat")
            W("vector", "s_pa", pa0 + 15)
            emit("vector", lambda e: e.tensor_copy(xbb_s(0)[:], acc_t[:, 0, :]).then_inc(SEM["s_stat"], 1))
            INC("s_stat")
            for kc in range(16):
                W("tensor", "s_stat", min(stat0 + kc + 2, stat0 + 17))
                emit("tensor", lambda e, k=kc: e.matmul(ps_d[0][0:1, :], ones_b[:], xbb_s(k)[:], start=(k == 0), stop=(k == 15)))
                if kc < 15:
                    emit("tensor", lambda e, k=kc: e.matmul(ps_d[1][0:1, :], ones_b[:], sq_s(k)[:], start=(k == 0), stop=False).then_inc(SEM["s_pa"], 1))
                else:
                    emit("tensor", lambda e: e.matmul(ps_d[1][0:1, :], ones_b[:], sq_s(15)[:], start=False, stop=True).then_inc(SEM["s_pa"], 1))
                INC("s_pa")
            W("scalar", "s_pa", cnt["s_pa"])
            emit("scalar", lambda e: e.activation(s_mean[:], ps_d[0][0:1, :], AF.Identity, scale=1.0 / D).then_inc(SEM["s_stat"], 1))
            INC("s_stat")
            emit("scalar", lambda e: e.activation(s_ex2[:], ps_d[1][0:1, :], AF.Identity, scale=1.0 / D).then_inc(SEM["s_stat"], 1))
            sv2 = INC("s_stat")
            W("vector", "s_stat", sv2)
            emit("vector", lambda e: e.tensor_tensor(s_var[:], s_mean[:], s_mean[:], OP.mult))
            emit("vector", lambda e: e.tensor_tensor(s_var[:], s_ex2[:], s_var[:], OP.subtract).then_inc(SEM["s_stat"], 1))
            sv3 = INC("s_stat")
            W("scalar", "s_stat", sv3)
            emit("scalar", lambda e: e.activation(s_ex2[:], s_var[:], AF.Ln, bias=cst[0:1, C_EPS:C_EPS + 1]))
            bc = C_LN05 if fold05 else C_ZERO
            emit("scalar", lambda e, b=bc: e.activation(s_rstd[:], s_ex2[:], AF.Exp, scale=-0.5, bias=cst[0:1, b:b + 1]).then_inc(SEM["s_stat"], 1))
            sv4 = INC("s_stat")
            W("scalar", "s_stat", sv4)
            DMA("scalar", statH[0:1, :], s_mean[:], "s_brd")
            hb = DMA("scalar", statH[1:2, :], s_rstd[:], "s_brd")
            W("scalar", "s_brd", hb)
            DMA("scalar", mean_b[:], statH[0:1, :].unsqueeze(1).broadcast_to([1, 128, TB]), "s_brd")
            DMA("scalar", rstd_b[:], statH[1:2, :].unsqueeze(1).broadcast_to([1, 128, TB]), "s_brd")
            W("vector", "s_brd", cnt["s_brd"])
            for kc in range(16):
                emit("vector", lambda e, k=kc: e.tensor_tensor(thv(k)[:], acc_t[:, k, :], mean_b[:], OP.subtract))
                emit("vector", lambda e, k=kc: e.tensor_tensor(h_t[:, k, :], thv(k)[:], rstd_b[:], OP.mult).then_inc(SEM["s_h"], 1))
            return INC("s_h", 16)

        def emit_attn(p):
            pa0 = cnt["s_pa"]
            pd0 = cnt["s_pd"]
            ge0 = cnt["s_ge"]
            bt_done, ct_done = [], []
            for nsc in range(8):
                if nsc >= 2:
                    W("gpsimd", "s_pa", pa0 + nsc - 1)
                elif p >= 1:
                    W("gpsimd", "s_pa", 40 * (p - 1) + 16 + 7 + nsc)
                else:
                    W("gpsimd", "s_pa", pa0)
                bt_done.append(DMA("gpsimd", btsl(nsc)[:, :, :],
                                   BTp.rearrange("(c p) n -> p c n", p=128)[:, :, 128 * nsc:128 * nsc + 128], "s_bc"))
            for dc in range(16):
                if dc >= 2:
                    W("gpsimd", "s_pd", pd0 + dc - 1)
                elif p >= 1:
                    W("gpsimd", "s_pd", 272 * (p - 1) + 15 + dc)
                else:
                    W("gpsimd", "s_pd", pd0)
                ct_done.append(DMA("gpsimd", ctsl(dc)[:, :, :],
                                   CTp.rearrange("(c p) n -> p c n", p=128)[:, :, 128 * dc:128 * dc + 128], "s_bc"))
            all_bc = DMA("gpsimd", f32pad[:], gsT[0:1, 0:64], "s_bc")

            W("tensor", "s_h", hist["hv1"])
            tanh_ge = []
            for nsc in range(8):
                bank = ps_g[nsc % 3]
                W("tensor", "s_bc", bt_done[nsc + 1] if nsc + 1 < 8 else ct_done[0])
                W("tensor", "s_ge", tanh_ge[nsc - 3] if nsc >= 3 else ge0)
                for kc in range(16):
                    if kc < 15:
                        emit("tensor", lambda e, k=kc, n=nsc, b=bank: e.matmul(b[:], btsl(n)[:, k, :], h_t[:, k, :], start=(k == 0), stop=False))
                    else:
                        emit("tensor", lambda e, n=nsc, b=bank: e.matmul(b[:], btsl(n)[:, 15, :], h_t[:, 15, :], start=False, stop=True).then_inc(SEM["s_pa"], 1))
                pav = INC("s_pa")
                W("scalar", "s_pa", pav)
                emit("scalar", lambda e, n=nsc, b=bank: e.activation(stb_t[:, n, :], b[:], AF.Tanh).then_inc(SEM["s_ge"], 1))
                tanh_ge.append(INC("s_ge"))
            hist["tanh_last"] = cnt["s_ge"]
            ev0 = cnt["s_ev"]
            attn_ev = []
            W("tensor", "s_ge", tanh_ge[7])
            for dc in range(16):
                bank = ps_d[dc % 2]
                W("tensor", "s_bc", ct_done[dc + 1] if dc + 1 < 16 else all_bc)
                W("tensor", "s_ev", attn_ev[dc - 2] if dc >= 2 else ev0)
                for nsc in range(8):
                    if nsc < 7:
                        emit("tensor", lambda e, n=nsc, d2=dc, b=bank: e.matmul(b[:], ctsl(d2)[:, n, :], stb_t[:, n, :], start=(n == 0), stop=False))
                    else:
                        emit("tensor", lambda e, d2=dc, b=bank: e.matmul(b[:], ctsl(d2)[:, 7, :], stb_t[:, 7, :], start=False, stop=True).then_inc(SEM["s_pd"], 1))
                pdv = INC("s_pd")
                W("vector", "s_pd", pdv)
                emit("vector", lambda e, d2=dc, b=bank: e.tensor_tensor(acc_t[:, d2, :], acc_t[:, d2, :], b[:], OP.add))
                emit("vector", lambda e, d2=dc: e.scalar_tensor_tensor(acc_t[:, d2, :], h_t[:, d2, :], dv[:, d2:d2 + 1], acc_t[:, d2, :], OP.mult, OP.add).then_inc(SEM["s_ev"], 1))
                attn_ev.append(INC("s_ev"))
            hist["attn_ev"] = attn_ev

        def emit_pass_prologue(p):
            tsl = slice(TB * p, TB * (p + 1))
            W("sync", "s_st", cnt["s_st"])
            xv = DMA("sync", acc_t[:, :, :], xTs.rearrange("(c p) t -> p c t", p=128)[:, :, tsl], "s_x")
            W("vector", "s_x", xv)
            W("vector", "s_pd", cnt["s_pd"])
            W("vector", "s_pg", cnt["s_pg"])
            W("vector", "s_puu", cnt["s_puu"])
            hist["hv1"] = layer_norm(False)
            emit_attn(p)
            W("vector", "s_ev", hist["attn_ev"][15])
            hist["hv2"] = layer_norm(True)
            W("tensor", "s_h", hist["hv2"])

        def emit_unit_loads(p, j, chase, wd_only=False, skip_wd=False):
            uidx = p * NU + j
            tsl_n = slice(UW * j, UW * (j + 1))
            wslot = j % 2
            if uidx >= 2:
                W("sync", "s_pg", hist["ugate"][uidx - 2])
                W("sync", "s_puu", hist["uup"][uidx - 2])
            e2 = 2 * j
            if e2 >= 10:
                W("sync", "s_pd", hist["gpd"][16 * p + (e2 - 10) // GSZ])
            elif p >= 1:
                g = 15 if e2 < 8 else 14
                W("sync", "s_pd", 272 * (p - 1) + 16 + 16 * (g + 1))
            if chase:
                W("sync", "s_wr", unit_wr_need(j))
            DMA("sync", wg_u(wslot)[:, :, :], wgS[:, :, tsl_n].rearrange("k p n -> p k n"), "s_ul")
            DMA("sync", wu_u(wslot)[:, :, :], wuS[:, :, tsl_n].rearrange("k p n -> p k n"), "s_ul")
            DMA("sync", wd_pair(2 * j)[:, :, :], wdS[:, 2 * j:2 * j + 2, :], "s_ul")

        def emit_ffn_unit(p, j, chase=False, loads=True):
            uidx = hist["unit"]
            wslot = j % 2
            if loads:
                emit_unit_loads(p, j, chase)
            W("tensor", "s_ul", 48 * (uidx + 1))
            for ep in range(EPU):
                E = hist["eta"]
                eloc = E % 128
                gb = ps_g[E % 3]
                ub = ps_u[E % 3]
                geg = max(hist["gelu"][E - 3] if E >= 3 else 0, hist["tanh_last"])
                W("tensor", "s_ge", geg)
                for kk in range(16):
                    if kk < 15:
                        emit("tensor", lambda e, k=kk, pp=ep, b=gb, ws=wslot: e.matmul(b[:], wg_u(ws)[:, k, 128 * pp:128 * pp + 128], h_t[:, k, :], start=(k == 0), stop=False))
                    else:
                        emit("tensor", lambda e, pp=ep, b=gb, ws=wslot: e.matmul(b[:], wg_u(ws)[:, 15, 128 * pp:128 * pp + 128], h_t[:, 15, :], start=False, stop=True).then_inc(SEM["s_pg"], 1))
                pgv = INC("s_pg")
                W("tensor", "s_it", hist["it"][E - 3] if E >= 3 else 0)
                for kk in range(16):
                    if kk < 15:
                        emit("tensor", lambda e, k=kk, pp=ep, b=ub, ws=wslot: e.matmul(b[:], wu_u(ws)[:, k, 128 * pp:128 * pp + 128], h_t[:, k, :], start=(k == 0), stop=False))
                    else:
                        emit("tensor", lambda e, pp=ep, b=ub, ws=wslot: e.matmul(b[:], wu_u(ws)[:, 15, 128 * pp:128 * pp + 128], h_t[:, 15, :], start=False, stop=True).then_inc(SEM["s_puu"], 1))
                puv = INC("s_puu")
                W("scalar", "s_pg", pgv)
                W("scalar", "s_it", hist["it"][E - 3] if E >= 3 else 0)
                emit("scalar", lambda e, b=gb, g=g1_t(E): e.activation(g[:], b[:], AF.Gelu_apprx_tanh).then_inc(SEM["s_ge"], 1))
                hist["gelu"].append(INC("s_ge"))
                W("vector", "s_ge", hist["gelu"][E])
                W("vector", "s_puu", puv)
                if eloc >= GSZ:
                    W("vector", "s_pd", hist["gpd"][-1])
                elif p > 0 and hist["gpd"]:
                    W("vector", "s_pd", hist["gpd"][-1])
                emit("vector", lambda e, b=ub, g=g1_t(E), s2=eloc % GSZ: e.scalar_tensor_tensor(inter_t[:, s2, :], g[:], 0.5, b[:], OP.mult, OP.mult).then_inc(SEM["s_it"], 1))
                hist["it"].append(INC("s_it"))
                hist["eta"] += 1
            hist["ugate"].append(cnt["s_pg"])
            hist["uup"].append(cnt["s_puu"])
            hist["unit"] = uidx + 1

            if j % 4 == 3:
                W("tensor", "s_it", hist["it"][hist["eta"] - 1])
                for hc in range(16):
                    db = ps_d[hc % 2]
                    li = len(hist["dnev"])
                    evg = hist["attn_ev"][14 + (hc % 2)]
                    if li >= 2:
                        evg = max(evg, hist["dnev"][li - 2])
                    W("tensor", "s_ev", evg)
                    for ei in range(GSZ):
                        egl = (hist["eta"] - GSZ) + ei
                        eloc2 = egl % 128
                        esl = (2 * (eloc2 // 2)) % 10 + (eloc2 % 2)
                        if ei < GSZ - 1:
                            emit("tensor", lambda e, s3=esl, h2=hc, s2=ei, b=db: e.matmul(b[:], wd_e(s3)[:, 128 * h2:128 * h2 + 128], inter_t[:, s2, :], start=(s2 == 0), stop=False))
                        else:
                            emit("tensor", lambda e, s3=esl, h2=hc, s2=ei, b=db: e.matmul(b[:], wd_e(s3)[:, 128 * h2:128 * h2 + 128], inter_t[:, s2, :], start=False, stop=True).then_inc(SEM["s_pd"], 1))
                    pdv = INC("s_pd")
                    W("vector", "s_pd", pdv)
                    emit("vector", lambda e, h2=hc, b=db: e.tensor_tensor(acc_t[:, h2, :], acc_t[:, h2, :], b[:], OP.add).then_inc(SEM["s_ev"], 1))
                    hist["dnev"].append(INC("s_ev"))
                hist["gpd"].append(cnt["s_pd"])

        def emit_store(p):
            tsl = slice(TB * p, TB * (p + 1))
            W("sync", "s_ev", hist["dnev"][-1])
            DMA("sync", oT.rearrange("(c p) t -> p c t", p=128)[:, :, tsl], acc_t[:, :, :], "s_st")

        emit_pass_prologue(0)
        emit_decode_prologue()
        ti = 0
        uj = 0
        for nr in range(NR):
            for _ in range(TPG):
                emit_decode_tile(ti)
                ti += 1
            if nr >= LAG_NR:
                tgt = 2 * (nr - LAG_NR) + 2
                while uj < min(tgt, NU):
                    emit_ffn_unit(0, uj, chase=True)
                    uj += 1
        emit_decode_tail()
        while uj < NU:
            emit_ffn_unit(0, uj, chase=True)
            uj += 1
        emit_unit_loads(1, 0, chase=True)
        emit_unit_loads(1, 1, chase=True)
        emit_store(0)
        if DEBUG_DUMPS:
            for nm2 in ("dbg_wgS", "dbg_wuS", "dbg_wdS"):
                if nm2 in dbg:
                    src2 = {"dbg_wgS": wgS, "dbg_wuS": wuS, "dbg_wdS": wdS}[nm2]
                    W("gpsimd", "s_wr", 32 * NT)
                    emit("gpsimd", lambda e, o=dbg[nm2], i=src2, s=SEM["s_st"]: e.dma_start(out=o, in_=i).then_inc(s, 16))
                    INC("s_st", 16)

        for p in range(1, NBLK):
            emit_pass_prologue(p)
            for j in range(NU):
                emit_ffn_unit(p, j, chase=(p == 1), loads=(j >= 2))
            if p + 1 < NBLK:
                emit_unit_loads(p + 1, 0, chase=False)
                emit_unit_loads(p + 1, 1, chase=False)
            emit_store(p)

        W("gpsimd", "s_st", cnt["s_st"])
        W("gpsimd", "s_wr", 32 * NT + 64)

        with nc.Block() as block:
            @block.sync
            def _(eng):
                for fn in progs["sync"]:
                    fn(eng)

            @block.gpsimd
            def _(eng):
                for fn in progs["gpsimd"]:
                    fn(eng)

            @block.vector
            def _(eng):
                for fn in progs["vector"]:
                    fn(eng)

            @block.scalar
            def _(eng):
                for fn in progs["scalar"]:
                    fn(eng)

            @block.tensor
            def _(eng):
                for fn in progs["tensor"]:
                    fn(eng)

    return nc


def kernel(L0_attn_x, L0_attn_B, L0_attn_C, L0_attn_D,
           L0_ffn_gate_W_packed, L0_ffn_gate_scales,
           L0_ffn_up_W_packed, L0_ffn_up_scales,
           L0_ffn_down_W_packed, L0_ffn_down_scales):
    pd_ = _perm(D)
    pn = _perm(NM)

    x = np.ascontiguousarray(np.asarray(L0_attn_x, np.float32).reshape(-1, D).T[pd_])
    BTp = np.ascontiguousarray(np.asarray(L0_attn_B, np.float32).T[pd_])
    CTp = np.ascontiguousarray(np.asarray(L0_attn_C, np.float32).T[:, pd_])
    Dp = np.ascontiguousarray(np.asarray(L0_attn_D, np.float32)[pd_].reshape(16, 128).T)

    def pack_u8(Wp):
        w = np.ascontiguousarray(np.asarray(Wp))
        return w.view(np.uint8).reshape(w.shape[0], w.shape[1], 4)[:, :, 0]

    gpTh = np.ascontiguousarray(pack_u8(L0_ffn_gate_W_packed).T[:, pn])
    upTh = np.ascontiguousarray(pack_u8(L0_ffn_up_W_packed).T[:, pn])
    dpTh = np.ascontiguousarray(pack_u8(L0_ffn_down_W_packed).T[:, pd_])
    gsTh = np.ascontiguousarray(np.asarray(L0_ffn_gate_scales, np.float32).T[:, pn])
    usTh = np.ascontiguousarray(np.asarray(L0_ffn_up_scales, np.float32).T[:, pn])
    dsTh = np.ascontiguousarray(np.asarray(L0_ffn_down_scales, np.float32).T[:, pd_])

    cstv = np.zeros((128, 8), np.float32)
    cstv[:, C_M12] = -12.0
    cstv[:, C_M4] = -4.0
    cstv[:, C_B4] = 4.0
    cstv[:, C_B1275] = 127.5
    cstv[:, C_EPS] = 1e-5
    cstv[:, C_LN05] = LN05
    ones = np.ones((128, 1), np.float32)

    nc = build_program()
    in_maps = []
    for r in range(NCORES):
        in_maps.append(dict(
            xTs=np.ascontiguousarray(x[:, TPC * r:TPC * (r + 1)]),
            gpT=gpTh, upT=upTh, dpT=dpTh, gsT=gsTh, usT=usTh, dsT=dsTh,
            BTp=BTp, CTp=CTp, Dp=Dp, cstI=cstv, onesI=ones,
        ))
    res = run_bass_kernel_spmd(nc, in_maps, core_ids=list(range(NCORES)))
    outT = np.concatenate([r["oT"] for r in res.results], axis=1)
    inv = np.argsort(pd_)
    out = outT[inv].T.reshape(2, 8192, D)
    return np.ascontiguousarray(out, dtype=np.float32)


# version 14
# speedup vs baseline: 1.2058x; 1.1067x over previous
import contextlib
import numpy as np
import concourse.bass as bass
import concourse.mybir as mybir
from concourse.bass_utils import run_bass_kernel_spmd

F32 = mybir.dt.float32
BF16 = mybir.dt.bfloat16
U8 = mybir.dt.uint8
AF = mybir.ActivationFunctionType
OP = mybir.AluOpType

NCORES = 8
D = 2048
NS = 1024
NM = 16384
TPC = 2048
TB = 512
NBLK = 4
FD0 = 512
NR = NM // FD0
TPG = 24
NT = NR * TPG
UW = 256
NU = NM // UW
EPU = 2
GSZ = 8
LAG_NR = 2
LN05 = float(np.log(0.5))

C_M12, C_M4, C_B4, C_B1275, C_EPS, C_LN05, C_ZERO, C_SP = range(8)

DEBUG_DUMPS = False


def _perm(n):
    blk = np.concatenate([np.arange(0, 256, 2), np.arange(1, 256, 2)])
    return (np.arange(n // 256)[:, None] * 256 + blk[None, :]).ravel()


def build_program():
    nc = bass.Bass("TRN2", target_bir_lowering=False)

    xTs = nc.dram_tensor("xTs", [D, TPC], F32, kind="ExternalInput").ap()
    gpT = nc.dram_tensor("gpT", [1024, NM], U8, kind="ExternalInput").ap()
    upT = nc.dram_tensor("upT", [1024, NM], U8, kind="ExternalInput").ap()
    dpT = nc.dram_tensor("dpT", [8192, D], U8, kind="ExternalInput").ap()
    gsT = nc.dram_tensor("gsT", [64, NM], F32, kind="ExternalInput").ap()
    usT = nc.dram_tensor("usT", [64, NM], F32, kind="ExternalInput").ap()
    dsT = nc.dram_tensor("dsT", [512, D], F32, kind="ExternalInput").ap()
    BTp = nc.dram_tensor("BTp", [D, NS], F32, kind="ExternalInput").ap()
    CTp = nc.dram_tensor("CTp", [NS, D], F32, kind="ExternalInput").ap()
    Dp = nc.dram_tensor("Dp", [128, 16], F32, kind="ExternalInput").ap()
    cstI = nc.dram_tensor("cstI", [128, 8], F32, kind="ExternalInput").ap()
    onesI = nc.dram_tensor("onesI", [128, 1], F32, kind="ExternalInput").ap()
    oT = nc.dram_tensor("oT", [D, TPC], F32, kind="ExternalOutput").ap()

    wgS = nc.dram_tensor("wgS", [16, 128, NM], BF16, kind="Internal").ap()
    wuS = nc.dram_tensor("wuS", [16, 128, NM], BF16, kind="Internal").ap()
    wdS = nc.dram_tensor("wdS", [128, 128, D], BF16, kind="Internal").ap()
    statH = nc.dram_tensor("statH", [2, TB], F32, kind="Internal").ap()
    padH = nc.dram_tensor("padH", [1, 64], F32, kind="Internal").ap()

    dbg = {}
    if DEBUG_DUMPS:
        import os
        if os.environ.get("DBG_FULL_SCRATCH"):
            for nm, shp, dt2 in (("dbg_wgS", [16, 128, NM], BF16),
                                 ("dbg_wuS", [16, 128, NM], BF16),
                                 ("dbg_wdS", [128, 128, D], BF16)):
                dbg[nm] = nc.dram_tensor(nm, shp, dt2, kind="ExternalOutput").ap()

    with contextlib.ExitStack() as st_:
        def sbt(name, shape, dt):
            return st_.enter_context(nc.sbuf_tensor(name, shape, dt))

        arU8 = sbt("arU8", [128, 8192], U8)
        arBF = sbt("arBF", [128, 49152], BF16)
        arF32 = sbt("arF32", [128, 11264], F32)
        thA = sbt("thA", [128, 512], F32)
        h_sb = sbt("h_sb", [128, 8192], BF16)
        inter_sb = sbt("inter_sb", [128, 4096], BF16)
        g1_sb = sbt("g1_sb", [128, 3 * TB], BF16)
        stb_sb = sbt("stb_sb", [128, 4096], BF16)
        xbb_sb = sbt("xbb_sb", [128, 1024], BF16)
        sq_sb = sbt("sq_sb", [128, 1024], BF16)
        btsl_sb = sbt("btsl_sb", [128, 4096], BF16)
        ctsl_sb = sbt("ctsl_sb", [128, 2048], BF16)
        cst = sbt("cst", [128, 8], F32)
        dv = sbt("dv", [128, 16], F32)
        ones_f = sbt("ones_f", [128, 1], F32)
        ones_b = sbt("ones_b", [128, 1], BF16)
        u8pad = sbt("u8pad", [1, 64], U8)
        f32pad = sbt("f32pad", [1, 64], F32)
        s_mean = sbt("s_mean", [1, TB], F32)
        s_ex2 = sbt("s_ex2", [1, TB], F32)
        s_var = sbt("s_var", [1, TB], F32)
        s_rstd = s_var

        ps_g = [st_.enter_context(nc.psum_tensor(f"ps_g{i}", [128, TB], F32)) for i in range(3)]
        ps_u = [st_.enter_context(nc.psum_tensor(f"ps_u{i}", [128, TB], F32)) for i in range(3)]
        ps_d = [st_.enter_context(nc.psum_tensor(f"ps_d{i}", [128, TB], F32)) for i in range(2)]

        semnames = ["s_ld", "s_pk", "s_scl", "s_dva", "s_dvb", "s_dvc",
                    "s_aca", "s_poo", "s_wr", "s_x", "s_bc",
                    "s_stat", "s_brd", "s_h", "s_pg", "s_puu", "s_pd",
                    "s_pa", "s_ge", "s_it", "s_ev", "s_ul", "s_st", "s_pre"]
        SEM = {n: st_.enter_context(nc.semaphore(n)) for n in semnames}

        def pkv(i):
            o = (i % 6) * FD0
            return arU8[:, o:o + FD0]

        U8_T8, U8_HI8, U8_M8L, U8_M8H = 3072, 4096, 5120, 6656
        U8_RING = {U8_T8: 2, U8_HI8: 2, U8_M8L: 3, U8_M8H: 3}

        def u8v(base, i):
            o = base + (i % U8_RING[base]) * FD0
            return arU8[:, o:o + FD0]

        (K_MBL, K_MBH, K_SGL, K_SGH, K_R6L, K_R6H) = range(6)
        (K_QL, K_QH, K_VL) = range(6, 9)

        def bfv(kind, i):
            if kind < 6:
                o = (kind * 3 + (i % 3)) * FD0
            else:
                o = (18 + (kind - 6) * 2 + (i % 2)) * FD0
            return arBF[:, o:o + FD0]

        BF_WG, BF_WU, BF_WD = 12288, 20480, 28672

        def wg_u(slot):
            o = BF_WG + (slot % 2) * 4096
            return arBF[:, o:o + 4096].rearrange("p (k n) -> p k n", k=16)

        def wu_u(slot):
            o = BF_WU + (slot % 2) * 4096
            return arBF[:, o:o + 4096].rearrange("p (k n) -> p k n", k=16)

        def wd_e(eslot):
            o = BF_WD + (eslot % 10) * 2048
            return arBF[:, o:o + 2048]

        def wd_pair(ebase):
            o = BF_WD + (ebase % 10) * 2048
            return arBF[:, o:o + 4096].rearrange("p (e h) -> p e h", e=2)

        h_t = h_sb[:].rearrange("p (k t) -> p k t", k=16)
        inter_t = inter_sb[:].rearrange("p (e t) -> p e t", e=8)

        def g1_t(e):
            o = (e % 3) * TB
            return g1_sb[:, o:o + TB]

        stb_t = stb_sb[:].rearrange("p (k t) -> p k t", k=8)

        def xbb_s(k):
            o = (k % 2) * TB
            return xbb_sb[:, o:o + TB]

        def sq_s(k):
            o = (k % 2) * TB
            return sq_sb[:, o:o + TB]

        def btsl(s):
            o = (s % 2) * 2048
            return btsl_sb[:, o:o + 2048].rearrange("p (k n) -> p k n", k=16)

        def ctsl(s):
            o = (s % 2) * 1024
            return ctsl_sb[:, o:o + 1024].rearrange("p (k n) -> p k n", k=8)

        acc_t = arF32[:, 0:8192].rearrange("p (k t) -> p k t", k=16)

        def sclv(i):
            o = 8192 + (i % 4) * FD0
            return arF32[:, o:o + FD0]

        mean_b = arF32[:, 10240:10752]
        rstd_b = arF32[:, 10752:11264]

        def thv(k):
            return thA[:, 0:TB]

        progs = {e: [] for e in ("sync", "gpsimd", "vector", "scalar", "tensor")}
        cnt = {n: 0 for n in semnames}

        def emit(eng, fn):
            progs[eng].append(fn)

        def W(eng, sem, val):
            if val > 0:
                emit(eng, lambda e, s=SEM[sem], v=val: e.wait_ge(s, v))

        def INC(sem, k=1):
            cnt[sem] += k
            return cnt[sem]

        def DMA(eng, out, in_, sem):
            v = INC(sem, 16)
            emit(eng, lambda e, o=out, i=in_, s=SEM[sem]: e.dma_start(out=o, in_=i).then_inc(s, 16))
            return v

        DMA("sync", cst[:], cstI, "s_ld")
        DMA("sync", dv[:], Dp, "s_ld")
        DMA("sync", ones_f[:], onesI, "s_ld")
        W("vector", "s_ld", 48)
        emit("vector", lambda e: e.tensor_copy(ones_b[:], ones_f[:]).then_inc(SEM["s_pre"], 1))
        INC("s_pre")
        W("scalar", "s_ld", 48)
        W("gpsimd", "s_ld", 48)

        tiles = []
        for nr in range(NR):
            for c in range(8):
                tiles.append(("g", gpT, gsT, wgS, c, nr))
            for c in range(8):
                tiles.append(("u", upT, usT, wuS, c, nr))
            for c2 in range(2):
                cd = 2 * nr + c2
                for hr in range(4):
                    tiles.append(("d", dpT, dsT, wdS, cd, hr))
        assert len(tiles) == NT

        def tile_srcs(i):
            kind, psrc, ssrc, dst, c, nr = tiles[i]
            pk_ap = psrc[128 * c:128 * (c + 1), FD0 * nr:FD0 * (nr + 1)]
            scl_ap = (ssrc[8 * c:8 * c + 8, FD0 * nr:FD0 * (nr + 1)]
                      .unsqueeze(1).broadcast_to([8, 16, FD0]))
            return pk_ap, scl_ap

        def tile_dsts(i):
            kind, psrc, ssrc, dst, c, nr = tiles[i]
            if kind in ("g", "u"):
                return (dst[2 * c, :, FD0 * nr:FD0 * (nr + 1)],
                        dst[2 * c + 1, :, FD0 * nr:FD0 * (nr + 1)])
            return (dst[:, 2 * c, FD0 * nr:FD0 * (nr + 1)],
                    dst[:, 2 * c + 1, FD0 * nr:FD0 * (nr + 1)])

        def emit_vecA(i):
            pk = pkv(i)
            W("vector", "s_pk", 16 * min(i + 3, NT + 2))
            emit("vector", lambda e, a=u8v(U8_T8, i), b=pk: e.tensor_scalar(a[:], b[:], 8, None, OP.bitwise_and).then_inc(SEM["s_dva"], 1))
            emit("vector", lambda e, a=u8v(U8_HI8, i), b=pk: e.tensor_scalar(a[:], b[:], 4, None, OP.logical_shift_right))
            emit("vector", lambda e, a=u8v(U8_M8L, i), b=pk: e.tensor_scalar(a[:], b[:], 7, None, OP.bitwise_and).then_inc(SEM["s_dva"], 1))
            emit("vector", lambda e, a=u8v(U8_M8H, i), b=u8v(U8_HI8, i): e.tensor_scalar(a[:], b[:], 7, None, OP.bitwise_and).then_inc(SEM["s_dva"], 1))
            emit("vector", lambda e, a=bfv(K_SGL, i), b=u8v(U8_T8, i): e.tensor_scalar(a[:], b[:], -0.25, 1.0, OP.mult, OP.add))

        def emit_scalA(i):
            pk = pkv(i)
            W("scalar", "s_dva", 3 * (i + 1))
            W("scalar", "s_pk", 16 * min(i + 3, NT + 2))
            if i >= 3:
                W("scalar", "s_dvb", i - 2)
                W("scalar", "s_dvc", i - 2)
                W("scalar", "s_wr", 32 * (i - 2))
            for fn_ in (
                lambda e, a=bfv(K_SGH, i), b=pk: e.activation(a[:], b[:], AF.Sign, scale=-1.0, bias=cst[:, C_B1275:C_B1275 + 1]).then_inc(SEM["s_aca"], 1),
                lambda e, a=bfv(K_R6L, i), b=u8v(U8_M8L, i): e.activation(a[:], b[:], AF.Relu, scale=2.0, bias=cst[:, C_M12:C_M12 + 1]).then_inc(SEM["s_aca"], 1),
                lambda e, a=bfv(K_R6H, i), b=u8v(U8_M8H, i): e.activation(a[:], b[:], AF.Relu, scale=2.0, bias=cst[:, C_M12:C_M12 + 1]).then_inc(SEM["s_aca"], 1),
                lambda e, a=bfv(K_MBL, i), b=u8v(U8_M8L, i): e.activation(a[:], b[:], AF.Identity).then_inc(SEM["s_aca"], 1),
                lambda e, a=bfv(K_MBH, i), b=u8v(U8_M8H, i): e.activation(a[:], b[:], AF.Identity).then_inc(SEM["s_aca"], 1),
            ):
                emit("scalar", fn_)

        def emit_decode_tile(i):
            pk_ap, _ = tile_srcs(i)
            if i >= 6:
                W("sync", "s_aca", 5 * (i - 5))
                W("sync", "s_dva", 3 * (i - 5))
            DMA("sync", pkv(i)[:], pk_ap, "s_pk")

            W("vector", "s_aca", 5 * (i + 1))
            if i >= 2:
                W("vector", "s_poo", i - 1)
                W("vector", "s_wr", 32 * (i - 1))
            emit("vector", lambda e, a=bfv(K_QL, i), b=bfv(K_R6L, i), c2=bfv(K_MBL, i): e.tensor_tensor(a[:], b[:], c2[:], OP.add))
            emit("vector", lambda e, a=bfv(K_QH, i), b=bfv(K_R6H, i), c2=bfv(K_MBH, i): e.tensor_tensor(a[:], b[:], c2[:], OP.add))
            emit("vector", lambda e, a=bfv(K_QL, i): e.tensor_scalar(a[:], a[:], -4.0, 0.0, OP.add, OP.max))
            emit("vector", lambda e, a=bfv(K_QH, i): e.tensor_scalar(a[:], a[:], -4.0, 0.0, OP.add, OP.max))
            emit("vector", lambda e, a=bfv(K_R6L, i), b=bfv(K_MBL, i), c2=bfv(K_QL, i): e.tensor_tensor(a[:], b[:], c2[:], OP.add))
            emit("vector", lambda e, a=bfv(K_R6H, i), b=bfv(K_MBH, i), c2=bfv(K_QH, i): e.tensor_tensor(a[:], b[:], c2[:], OP.add))
            emit("vector", lambda e, a=bfv(K_QH, i), b=bfv(K_R6H, i), c2=bfv(K_SGH, i): e.tensor_tensor(a[:], b[:], c2[:], OP.mult).then_inc(SEM["s_dvc"], 1))
            W("vector", "s_scl", 16 * min(i + 3, NT + 2))
            emit("vector", lambda e, a=bfv(K_VL, i), b=bfv(K_R6L, i), c2=bfv(K_SGL, i): e.tensor_tensor(a[:], b[:], c2[:], OP.mult))
            emit("vector", lambda e, a=bfv(K_VL, i), c2=sclv(i): e.tensor_tensor(a[:], a[:], c2[:], OP.mult).then_inc(SEM["s_dvb"], 1))
            emit("vector", lambda e, a=bfv(K_R6H, i), b=bfv(K_QH, i), c2=sclv(i): e.tensor_tensor(a[:], b[:], c2[:], OP.mult).then_inc(SEM["s_poo"], 1))
            if i + 3 < NT:
                emit_vecA(i + 3)

            emit_scalA(i)
            if i >= 2:
                lo_dst, hi_dst = tile_dsts(i - 2)
                W("scalar", "s_dvb", i - 1)
                W("scalar", "s_poo", i - 1)
                emit("scalar", lambda e, o=lo_dst, s2=bfv(K_VL, i - 2), s=SEM["s_wr"]: e.dma_start(out=o, in_=s2[:]).then_inc(s, 16))
                emit("scalar", lambda e, o=hi_dst, s2=bfv(K_R6H, i - 2), s=SEM["s_wr"]: e.dma_start(out=o, in_=s2[:]).then_inc(s, 16))

            if i + 3 < NT:
                _, scl_ap3 = tile_srcs(i + 3)
                if i >= 1:
                    W("gpsimd", "s_dvb", i)
                    W("gpsimd", "s_poo", i)
                DMA("gpsimd", sclv(i + 3)[:], scl_ap3, "s_scl")
            elif i + 3 in (NT, NT + 1):
                DMA("gpsimd", f32pad[:], gsT[0:1, 0:64], "s_scl")

        def emit_decode_prologue():
            for k in range(min(3, NT)):
                _, scl_ap0 = tile_srcs(k)
                DMA("gpsimd", sclv(k)[:], scl_ap0, "s_scl")
            for k in range(min(3, NT)):
                emit_vecA(k)

        def emit_decode_tail():
            for j in (NT - 2, NT - 1):
                lo_dst, hi_dst = tile_dsts(j)
                W("scalar", "s_dvb", j + 1)
                W("scalar", "s_poo", j + 1)
                emit("scalar", lambda e, o=lo_dst, s2=bfv(K_VL, j), s=SEM["s_wr"]: e.dma_start(out=o, in_=s2[:]).then_inc(s, 16))
                emit("scalar", lambda e, o=hi_dst, s2=bfv(K_R6H, j), s=SEM["s_wr"]: e.dma_start(out=o, in_=s2[:]).then_inc(s, 16))
            for dk in range(4):
                emit("scalar", lambda e, k=dk, s=SEM["s_wr"]: e.dma_start(out=padH[0:1, 16 * k:16 * (k + 1)], in_=f32pad[0:1, 0:16]).then_inc(s, 16))
            cnt["s_wr"] = 32 * NT + 64
            DMA("sync", u8pad[:], gpT[0:1, 0:64], "s_pk")
            DMA("sync", u8pad[:], gpT[0:1, 64:128], "s_pk")

        def unit_wr_need(j):
            return min(32 * TPG * (j // 2 + 2), 32 * NT + 64)

        hist = dict(gelu=[], it=[], ugate=[], uup=[], gpd=[], dnev=[],
                    eta=0, unit=0, tanh_last=0, attn_ev=None, hv1=0, hv2=0)

        def layer_norm(fold05):
            ev0 = cnt["s_ev"]
            W("tensor", "s_ev", ev0)
            W("tensor", "s_pre", 1)
            stat0 = cnt["s_stat"]
            pa0 = cnt["s_pa"]
            for kc in range(16):
                if kc >= 2:
                    W("vector", "s_pa", pa0 + kc - 1)
                emit("vector", lambda e, k=kc: e.tensor_copy(xbb_s(k)[:], acc_t[:, k, :]))
                emit("vector", lambda e, k=kc: e.tensor_tensor(sq_s(k)[:], xbb_s(k)[:], xbb_s(k)[:], OP.mult).then_inc(SEM["s_stat"], 1))
                INC("s_stat")
            W("vector", "s_pa", pa0 + 15)
            emit("vector", lambda e: e.tensor_copy(xbb_s(0)[:], acc_t[:, 0, :]).then_inc(SEM["s_stat"], 1))
            INC("s_stat")
            for kc in range(16):
                W("tensor", "s_stat", min(stat0 + kc + 2, stat0 + 17))
                emit("tensor", lambda e, k=kc: e.matmul(ps_d[0][0:1, :], ones_b[:], xbb_s(k)[:], start=(k == 0), stop=(k == 15)))
                if kc < 15:
                    emit("tensor", lambda e, k=kc: e.matmul(ps_d[1][0:1, :], ones_b[:], sq_s(k)[:], start=(k == 0), stop=False).then_inc(SEM["s_pa"], 1))
                else:
                    emit("tensor", lambda e: e.matmul(ps_d[1][0:1, :], ones_b[:], sq_s(15)[:], start=False, stop=True).then_inc(SEM["s_pa"], 1))
                INC("s_pa")
            W("scalar", "s_pa", cnt["s_pa"])
            emit("scalar", lambda e: e.activation(s_mean[:], ps_d[0][0:1, :], AF.Identity, scale=1.0 / D).then_inc(SEM["s_stat"], 1))
            INC("s_stat")
            emit("scalar", lambda e: e.activation(s_ex2[:], ps_d[1][0:1, :], AF.Identity, scale=1.0 / D).then_inc(SEM["s_stat"], 1))
            sv2 = INC("s_stat")
            W("vector", "s_stat", sv2)
            emit("vector", lambda e: e.tensor_tensor(s_var[:], s_mean[:], s_mean[:], OP.mult))
            emit("vector", lambda e: e.tensor_tensor(s_var[:], s_ex2[:], s_var[:], OP.subtract).then_inc(SEM["s_stat"], 1))
            sv3 = INC("s_stat")
            W("scalar", "s_stat", sv3)
            emit("scalar", lambda e: e.activation(s_ex2[:], s_var[:], AF.Ln, bias=cst[0:1, C_EPS:C_EPS + 1]))
            bc = C_LN05 if fold05 else C_ZERO
            emit("scalar", lambda e, b=bc: e.activation(s_rstd[:], s_ex2[:], AF.Exp, scale=-0.5, bias=cst[0:1, b:b + 1]).then_inc(SEM["s_stat"], 1))
            sv4 = INC("s_stat")
            W("scalar", "s_stat", sv4)
            DMA("scalar", statH[0:1, :], s_mean[:], "s_brd")
            hb = DMA("scalar", statH[1:2, :], s_rstd[:], "s_brd")
            W("scalar", "s_brd", hb)
            DMA("scalar", mean_b[:], statH[0:1, :].unsqueeze(1).broadcast_to([1, 128, TB]), "s_brd")
            DMA("scalar", rstd_b[:], statH[1:2, :].unsqueeze(1).broadcast_to([1, 128, TB]), "s_brd")
            W("vector", "s_brd", cnt["s_brd"])
            for kc in range(16):
                emit("vector", lambda e, k=kc: e.tensor_tensor(thv(k)[:], acc_t[:, k, :], mean_b[:], OP.subtract))
                emit("vector", lambda e, k=kc: e.tensor_tensor(h_t[:, k, :], thv(k)[:], rstd_b[:], OP.mult).then_inc(SEM["s_h"], 1))
            return INC("s_h", 16)

        def emit_attn(p):
            pa0 = cnt["s_pa"]
            pd0 = cnt["s_pd"]
            ge0 = cnt["s_ge"]
            bt_done, ct_done = [], []
            for nsc in range(8):
                if nsc >= 2:
                    W("gpsimd", "s_pa", pa0 + nsc - 1)
                elif p >= 1:
                    W("gpsimd", "s_pa", 40 * (p - 1) + 16 + 7 + nsc)
                else:
                    W("gpsimd", "s_pa", pa0)
                bt_done.append(DMA("gpsimd", btsl(nsc)[:, :, :],
                                   BTp.rearrange("(c p) n -> p c n", p=128)[:, :, 128 * nsc:128 * nsc + 128], "s_bc"))
            for dc in range(16):
                if dc >= 2:
                    W("gpsimd", "s_pd", pd0 + dc - 1)
                elif p >= 1:
                    W("gpsimd", "s_pd", 272 * (p - 1) + 15 + dc)
                else:
                    W("gpsimd", "s_pd", pd0)
                ct_done.append(DMA("gpsimd", ctsl(dc)[:, :, :],
                                   CTp.rearrange("(c p) n -> p c n", p=128)[:, :, 128 * dc:128 * dc + 128], "s_bc"))
            all_bc = DMA("gpsimd", f32pad[:], gsT[0:1, 0:64], "s_bc")

            W("tensor", "s_h", hist["hv1"])
            tanh_ge = []
            for nsc in range(8):
                bank = ps_g[nsc % 3]
                W("tensor", "s_bc", bt_done[nsc + 1] if nsc + 1 < 8 else ct_done[0])
                W("tensor", "s_ge", tanh_ge[nsc - 3] if nsc >= 3 else ge0)
                for kc in range(16):
                    if kc < 15:
                        emit("tensor", lambda e, k=kc, n=nsc, b=bank: e.matmul(b[:], btsl(n)[:, k, :], h_t[:, k, :], start=(k == 0), stop=False))
                    else:
                        emit("tensor", lambda e, n=nsc, b=bank: e.matmul(b[:], btsl(n)[:, 15, :], h_t[:, 15, :], start=False, stop=True).then_inc(SEM["s_pa"], 1))
                pav = INC("s_pa")
                W("scalar", "s_pa", pav)
                emit("scalar", lambda e, n=nsc, b=bank: e.activation(stb_t[:, n, :], b[:], AF.Tanh).then_inc(SEM["s_ge"], 1))
                tanh_ge.append(INC("s_ge"))
            hist["tanh_last"] = cnt["s_ge"]
            ev0 = cnt["s_ev"]
            attn_ev = []
            W("tensor", "s_ge", tanh_ge[7])
            for dc in range(16):
                bank = ps_d[dc % 2]
                W("tensor", "s_bc", ct_done[dc + 1] if dc + 1 < 16 else all_bc)
                W("tensor", "s_ev", attn_ev[dc - 2] if dc >= 2 else ev0)
                for nsc in range(8):
                    if nsc < 7:
                        emit("tensor", lambda e, n=nsc, d2=dc, b=bank: e.matmul(b[:], ctsl(d2)[:, n, :], stb_t[:, n, :], start=(n == 0), stop=False))
                    else:
                        emit("tensor", lambda e, d2=dc, b=bank: e.matmul(b[:], ctsl(d2)[:, 7, :], stb_t[:, 7, :], start=False, stop=True).then_inc(SEM["s_pd"], 1))
                pdv = INC("s_pd")
                W("vector", "s_pd", pdv)
                emit("vector", lambda e, d2=dc, b=bank: e.tensor_tensor(acc_t[:, d2, :], acc_t[:, d2, :], b[:], OP.add))
                emit("vector", lambda e, d2=dc: e.scalar_tensor_tensor(acc_t[:, d2, :], h_t[:, d2, :], dv[:, d2:d2 + 1], acc_t[:, d2, :], OP.mult, OP.add).then_inc(SEM["s_ev"], 1))
                attn_ev.append(INC("s_ev"))
            hist["attn_ev"] = attn_ev

        def emit_pass_prologue(p):
            tsl = slice(TB * p, TB * (p + 1))
            W("sync", "s_st", cnt["s_st"])
            xv = DMA("sync", acc_t[:, :, :], xTs.rearrange("(c p) t -> p c t", p=128)[:, :, tsl], "s_x")
            W("vector", "s_x", xv)
            W("vector", "s_pd", cnt["s_pd"])
            W("vector", "s_pg", cnt["s_pg"])
            W("vector", "s_puu", cnt["s_puu"])
            hist["hv1"] = layer_norm(False)
            emit_attn(p)
            W("vector", "s_ev", hist["attn_ev"][15])
            hist["hv2"] = layer_norm(True)
            W("tensor", "s_h", hist["hv2"])

        def emit_unit_loads(p, j, chase, wd_only=False, skip_wd=False):
            uidx = p * NU + j
            tsl_n = slice(UW * j, UW * (j + 1))
            wslot = j % 2
            if uidx >= 2:
                W("sync", "s_pg", hist["ugate"][uidx - 2])
                W("sync", "s_puu", hist["uup"][uidx - 2])
            e2 = 2 * j
            if e2 >= 10:
                W("sync", "s_pd", hist["gpd"][16 * p + (e2 - 10) // GSZ])
            elif p >= 1:
                g = 15 if e2 < 8 else 14
                W("sync", "s_pd", 272 * (p - 1) + 16 + 16 * (g + 1))
            if chase:
                W("sync", "s_wr", unit_wr_need(j))
            DMA("sync", wg_u(wslot)[:, :, :], wgS[:, :, tsl_n].rearrange("k p n -> p k n"), "s_ul")
            DMA("sync", wu_u(wslot)[:, :, :], wuS[:, :, tsl_n].rearrange("k p n -> p k n"), "s_ul")
            DMA("sync", wd_pair(2 * j)[:, :, :], wdS[:, 2 * j:2 * j + 2, :], "s_ul")

        def emit_ffn_unit(p, j, chase=False, loads=True):
            uidx = hist["unit"]
            wslot = j % 2
            if loads:
                emit_unit_loads(p, j, chase)
            W("tensor", "s_ul", 48 * (uidx + 1))
            for ep in range(EPU):
                E = hist["eta"]
                eloc = E % 128
                gb = ps_g[E % 3]
                ub = ps_u[E % 3]
                geg = max(hist["gelu"][E - 3] if E >= 3 else 0, hist["tanh_last"])
                W("tensor", "s_ge", geg)
                for kk in range(16):
                    if kk < 15:
                        emit("tensor", lambda e, k=kk, pp=ep, b=gb, ws=wslot: e.matmul(b[:], wg_u(ws)[:, k, 128 * pp:128 * pp + 128], h_t[:, k, :], start=(k == 0), stop=False))
                    else:
                        emit("tensor", lambda e, pp=ep, b=gb, ws=wslot: e.matmul(b[:], wg_u(ws)[:, 15, 128 * pp:128 * pp + 128], h_t[:, 15, :], start=False, stop=True).then_inc(SEM["s_pg"], 1))
                pgv = INC("s_pg")
                W("tensor", "s_it", hist["it"][E - 3] if E >= 3 else 0)
                for kk in range(16):
                    if kk < 15:
                        emit("tensor", lambda e, k=kk, pp=ep, b=ub, ws=wslot: e.matmul(b[:], wu_u(ws)[:, k, 128 * pp:128 * pp + 128], h_t[:, k, :], start=(k == 0), stop=False))
                    else:
                        emit("tensor", lambda e, pp=ep, b=ub, ws=wslot: e.matmul(b[:], wu_u(ws)[:, 15, 128 * pp:128 * pp + 128], h_t[:, 15, :], start=False, stop=True).then_inc(SEM["s_puu"], 1))
                puv = INC("s_puu")
                W("scalar", "s_pg", pgv)
                W("scalar", "s_it", hist["it"][E - 3] if E >= 3 else 0)
                emit("scalar", lambda e, b=gb, g=g1_t(E): e.activation(g[:], b[:], AF.Gelu_apprx_tanh).then_inc(SEM["s_ge"], 1))
                hist["gelu"].append(INC("s_ge"))
                W("vector", "s_ge", hist["gelu"][E])
                W("vector", "s_puu", puv)
                if eloc >= GSZ:
                    W("vector", "s_pd", hist["gpd"][-1])
                elif p > 0 and hist["gpd"]:
                    W("vector", "s_pd", hist["gpd"][-1])
                emit("vector", lambda e, b=ub, g=g1_t(E), s2=eloc % GSZ: e.scalar_tensor_tensor(inter_t[:, s2, :], g[:], 0.5, b[:], OP.mult, OP.mult).then_inc(SEM["s_it"], 1))
                hist["it"].append(INC("s_it"))
                hist["eta"] += 1
            hist["ugate"].append(cnt["s_pg"])
            hist["uup"].append(cnt["s_puu"])
            hist["unit"] = uidx + 1

            if j % 4 == 3:
                W("tensor", "s_it", hist["it"][hist["eta"] - 1])
                for hc in range(16):
                    db = ps_d[hc % 2]
                    li = len(hist["dnev"])
                    evg = hist["attn_ev"][14 + (hc % 2)]
                    if li >= 2:
                        evg = max(evg, hist["dnev"][li - 2])
                    W("tensor", "s_ev", evg)
                    for ei in range(GSZ):
                        egl = (hist["eta"] - GSZ) + ei
                        eloc2 = egl % 128
                        esl = (2 * (eloc2 // 2)) % 10 + (eloc2 % 2)
                        if ei < GSZ - 1:
                            emit("tensor", lambda e, s3=esl, h2=hc, s2=ei, b=db: e.matmul(b[:], wd_e(s3)[:, 128 * h2:128 * h2 + 128], inter_t[:, s2, :], start=(s2 == 0), stop=False))
                        else:
                            emit("tensor", lambda e, s3=esl, h2=hc, s2=ei, b=db: e.matmul(b[:], wd_e(s3)[:, 128 * h2:128 * h2 + 128], inter_t[:, s2, :], start=False, stop=True).then_inc(SEM["s_pd"], 1))
                    pdv = INC("s_pd")
                    W("vector", "s_pd", pdv)
                    emit("vector", lambda e, h2=hc, b=db: e.tensor_tensor(acc_t[:, h2, :], acc_t[:, h2, :], b[:], OP.add).then_inc(SEM["s_ev"], 1))
                    hist["dnev"].append(INC("s_ev"))
                hist["gpd"].append(cnt["s_pd"])

        def emit_store(p):
            tsl = slice(TB * p, TB * (p + 1))
            W("sync", "s_ev", hist["dnev"][-1])
            DMA("sync", oT.rearrange("(c p) t -> p c t", p=128)[:, :, tsl], acc_t[:, :, :], "s_st")

        emit_pass_prologue(0)
        emit_decode_prologue()
        ti = 0
        uj = 0
        for nr in range(NR):
            for _ in range(TPG):
                emit_decode_tile(ti)
                ti += 1
            if nr >= LAG_NR:
                tgt = 2 * (nr - LAG_NR) + 2
                while uj < min(tgt, NU):
                    emit_ffn_unit(0, uj, chase=True)
                    uj += 1
        emit_decode_tail()
        while uj < NU:
            emit_ffn_unit(0, uj, chase=True)
            uj += 1
        emit_unit_loads(1, 0, chase=True)
        emit_unit_loads(1, 1, chase=True)
        emit_store(0)
        if DEBUG_DUMPS:
            for nm2 in ("dbg_wgS", "dbg_wuS", "dbg_wdS"):
                if nm2 in dbg:
                    src2 = {"dbg_wgS": wgS, "dbg_wuS": wuS, "dbg_wdS": wdS}[nm2]
                    W("gpsimd", "s_wr", 32 * NT)
                    emit("gpsimd", lambda e, o=dbg[nm2], i=src2, s=SEM["s_st"]: e.dma_start(out=o, in_=i).then_inc(s, 16))
                    INC("s_st", 16)

        for p in range(1, NBLK):
            emit_pass_prologue(p)
            for j in range(NU):
                emit_ffn_unit(p, j, chase=(p == 1), loads=(j >= 2))
            if p + 1 < NBLK:
                emit_unit_loads(p + 1, 0, chase=False)
                emit_unit_loads(p + 1, 1, chase=False)
            emit_store(p)

        W("gpsimd", "s_st", cnt["s_st"])
        W("gpsimd", "s_wr", 32 * NT + 64)

        with nc.Block() as block:
            @block.sync
            def _(eng):
                for fn in progs["sync"]:
                    fn(eng)

            @block.gpsimd
            def _(eng):
                for fn in progs["gpsimd"]:
                    fn(eng)

            @block.vector
            def _(eng):
                for fn in progs["vector"]:
                    fn(eng)

            @block.scalar
            def _(eng):
                for fn in progs["scalar"]:
                    fn(eng)

            @block.tensor
            def _(eng):
                for fn in progs["tensor"]:
                    fn(eng)

    return nc


def kernel(L0_attn_x, L0_attn_B, L0_attn_C, L0_attn_D,
           L0_ffn_gate_W_packed, L0_ffn_gate_scales,
           L0_ffn_up_W_packed, L0_ffn_up_scales,
           L0_ffn_down_W_packed, L0_ffn_down_scales):
    pd_ = _perm(D)
    pn = _perm(NM)

    x = np.ascontiguousarray(np.asarray(L0_attn_x, np.float32).reshape(-1, D).T[pd_])
    BTp = np.ascontiguousarray(np.asarray(L0_attn_B, np.float32).T[pd_])
    CTp = np.ascontiguousarray(np.asarray(L0_attn_C, np.float32).T[:, pd_])
    Dp = np.ascontiguousarray(np.asarray(L0_attn_D, np.float32)[pd_].reshape(16, 128).T)

    def pack_u8(Wp):
        w = np.ascontiguousarray(np.asarray(Wp))
        return w.view(np.uint8).reshape(w.shape[0], w.shape[1], 4)[:, :, 0]

    gpTh = np.ascontiguousarray(pack_u8(L0_ffn_gate_W_packed).T[:, pn])
    upTh = np.ascontiguousarray(pack_u8(L0_ffn_up_W_packed).T[:, pn])
    dpTh = np.ascontiguousarray(pack_u8(L0_ffn_down_W_packed).T[:, pd_])
    gsTh = np.ascontiguousarray(np.asarray(L0_ffn_gate_scales, np.float32).T[:, pn])
    usTh = np.ascontiguousarray(np.asarray(L0_ffn_up_scales, np.float32).T[:, pn])
    dsTh = np.ascontiguousarray(np.asarray(L0_ffn_down_scales, np.float32).T[:, pd_])

    cstv = np.zeros((128, 8), np.float32)
    cstv[:, C_M12] = -12.0
    cstv[:, C_M4] = -4.0
    cstv[:, C_B4] = 4.0
    cstv[:, C_B1275] = 127.5
    cstv[:, C_EPS] = 1e-5
    cstv[:, C_LN05] = LN05
    ones = np.ones((128, 1), np.float32)

    nc = build_program()
    in_maps = []
    for r in range(NCORES):
        in_maps.append(dict(
            xTs=np.ascontiguousarray(x[:, TPC * r:TPC * (r + 1)]),
            gpT=gpTh, upT=upTh, dpT=dpTh, gsT=gsTh, usT=usTh, dsT=dsTh,
            BTp=BTp, CTp=CTp, Dp=Dp, cstI=cstv, onesI=ones,
        ))
    res = run_bass_kernel_spmd(nc, in_maps, core_ids=list(range(NCORES)))
    outT = np.concatenate([r["oT"] for r in res.results], axis=1)
    inv = np.argsort(pd_)
    out = outT[inv].T.reshape(2, 8192, D)
    return np.ascontiguousarray(out, dtype=np.float32)


# version 15
# speedup vs baseline: 1.2314x; 1.0213x over previous
import contextlib
import numpy as np
import concourse.bass as bass
import concourse.mybir as mybir
from concourse.bass_utils import run_bass_kernel_spmd

F32 = mybir.dt.float32
BF16 = mybir.dt.bfloat16
U8 = mybir.dt.uint8
AF = mybir.ActivationFunctionType
OP = mybir.AluOpType

NCORES = 8
D = 2048
NS = 1024
NM = 16384
TPC = 2048
TB = 512
NBLK = 4
FD0 = 512
NR = NM // FD0
TPG = 24
NT = NR * TPG
UW = 256
NU = NM // UW
EPU = 2
GSZ = 8
LAG_NR = 2
LN05 = float(np.log(0.5))

C_M12, C_M4, C_B4, C_B1275, C_EPS, C_LN05, C_ZERO, C_SP = range(8)

DEBUG_DUMPS = False


def _perm(n):
    blk = np.concatenate([np.arange(0, 256, 2), np.arange(1, 256, 2)])
    return (np.arange(n // 256)[:, None] * 256 + blk[None, :]).ravel()


def build_program():
    nc = bass.Bass("TRN2", target_bir_lowering=False)

    xTs = nc.dram_tensor("xTs", [D, TPC], F32, kind="ExternalInput").ap()
    gpT = nc.dram_tensor("gpT", [1024, NM], U8, kind="ExternalInput").ap()
    upT = nc.dram_tensor("upT", [1024, NM], U8, kind="ExternalInput").ap()
    dpT = nc.dram_tensor("dpT", [8192, D], U8, kind="ExternalInput").ap()
    gsT = nc.dram_tensor("gsT", [64, NM], F32, kind="ExternalInput").ap()
    usT = nc.dram_tensor("usT", [64, NM], F32, kind="ExternalInput").ap()
    dsT = nc.dram_tensor("dsT", [512, D], F32, kind="ExternalInput").ap()
    BTp = nc.dram_tensor("BTp", [D, NS], F32, kind="ExternalInput").ap()
    CTp = nc.dram_tensor("CTp", [NS, D], F32, kind="ExternalInput").ap()
    Dp = nc.dram_tensor("Dp", [128, 16], F32, kind="ExternalInput").ap()
    cstI = nc.dram_tensor("cstI", [128, 8], F32, kind="ExternalInput").ap()
    onesI = nc.dram_tensor("onesI", [128, 1], F32, kind="ExternalInput").ap()
    oT = nc.dram_tensor("oT", [D, TPC], F32, kind="ExternalOutput").ap()

    wgS = nc.dram_tensor("wgS", [16, 128, NM], BF16, kind="Internal").ap()
    wuS = nc.dram_tensor("wuS", [16, 128, NM], BF16, kind="Internal").ap()
    wdS = nc.dram_tensor("wdS", [128, 128, D], BF16, kind="Internal").ap()
    statH = nc.dram_tensor("statH", [2, TB], F32, kind="Internal").ap()
    padH = nc.dram_tensor("padH", [1, 64], F32, kind="Internal").ap()

    dbg = {}
    if DEBUG_DUMPS:
        import os
        if os.environ.get("DBG_FULL_SCRATCH"):
            for nm, shp, dt2 in (("dbg_wgS", [16, 128, NM], BF16),
                                 ("dbg_wuS", [16, 128, NM], BF16),
                                 ("dbg_wdS", [128, 128, D], BF16)):
                dbg[nm] = nc.dram_tensor(nm, shp, dt2, kind="ExternalOutput").ap()

    with contextlib.ExitStack() as st_:
        def sbt(name, shape, dt):
            return st_.enter_context(nc.sbuf_tensor(name, shape, dt))

        arU8 = sbt("arU8", [128, 8192], U8)
        arBF = sbt("arBF", [128, 49152], BF16)
        arF32 = sbt("arF32", [128, 11264], F32)
        thA = sbt("thA", [128, 512], F32)
        h_sb = sbt("h_sb", [128, 8192], BF16)
        inter_sb = sbt("inter_sb", [128, 4096], BF16)
        g1_sb = sbt("g1_sb", [128, 3 * TB], BF16)
        stb_sb = sbt("stb_sb", [128, 4096], BF16)
        xbb_sb = sbt("xbb_sb", [128, 1024], BF16)
        sq_sb = sbt("sq_sb", [128, 1024], BF16)
        btsl_sb = sbt("btsl_sb", [128, 4096], BF16)
        ctsl_sb = sbt("ctsl_sb", [128, 2048], BF16)
        cst = sbt("cst", [128, 8], F32)
        dv = sbt("dv", [128, 16], F32)
        ones_f = sbt("ones_f", [128, 1], F32)
        ones_b = sbt("ones_b", [128, 1], BF16)
        u8pad = sbt("u8pad", [1, 64], U8)
        f32pad = sbt("f32pad", [1, 64], F32)
        s_mean = sbt("s_mean", [1, TB], F32)
        s_ex2 = sbt("s_ex2", [1, TB], F32)
        s_var = sbt("s_var", [1, TB], F32)
        s_rstd = s_var

        ps_g = [st_.enter_context(nc.psum_tensor(f"ps_g{i}", [128, TB], F32)) for i in range(3)]
        ps_u = [st_.enter_context(nc.psum_tensor(f"ps_u{i}", [128, TB], F32)) for i in range(3)]
        ps_d = [st_.enter_context(nc.psum_tensor(f"ps_d{i}", [128, TB], F32)) for i in range(2)]

        semnames = ["s_ld", "s_pk", "s_scl", "s_dva", "s_dvb", "s_dvc",
                    "s_aca", "s_poo", "s_wr", "s_x", "s_bc",
                    "s_stat", "s_brd", "s_h", "s_pg", "s_puu", "s_pd",
                    "s_pa", "s_ge", "s_it", "s_ev", "s_ul", "s_st", "s_pre"]
        SEM = {n: st_.enter_context(nc.semaphore(n)) for n in semnames}

        def pkv(i):
            o = (i % 6) * FD0
            return arU8[:, o:o + FD0]

        U8_T8, U8_HI8, U8_M8L, U8_M8H = 3072, 4096, 5120, 5632
        U8_RING = {U8_T8: 2, U8_HI8: 2}

        def u8v(base, i):
            if base in (U8_M8L, U8_M8H):
                o = base + (i % 3) * 1024
            else:
                o = base + (i % U8_RING[base]) * FD0
            return arU8[:, o:o + FD0]

        def m8pair(i):
            o = 5120 + (i % 3) * 1024
            return arU8[:, o:o + 1024]

        (K_MBL, K_MBH, K_SGL, K_SGH, K_R6L, K_R6H) = range(6)
        (K_QL, K_QH, K_VL) = range(6, 9)

        def bfv(kind, i):
            if kind < 6:
                o = (i % 3) * 3072 + kind * FD0
            else:
                o = 9216 + (i % 2) * 1536 + (kind - 6) * FD0
            return arBF[:, o:o + FD0]

        def bfv2(kind, i):
            if kind < 6:
                o = (i % 3) * 3072 + kind * FD0
            else:
                o = 9216 + (i % 2) * 1536 + (kind - 6) * FD0
            return arBF[:, o:o + 2 * FD0]

        BF_WG, BF_WU, BF_WD = 12288, 20480, 28672

        def wg_u(slot):
            o = BF_WG + (slot % 2) * 4096
            return arBF[:, o:o + 4096].rearrange("p (k n) -> p k n", k=16)

        def wu_u(slot):
            o = BF_WU + (slot % 2) * 4096
            return arBF[:, o:o + 4096].rearrange("p (k n) -> p k n", k=16)

        def wd_e(eslot):
            o = BF_WD + (eslot % 10) * 2048
            return arBF[:, o:o + 2048]

        def wd_pair(ebase):
            o = BF_WD + (ebase % 10) * 2048
            return arBF[:, o:o + 4096].rearrange("p (e h) -> p e h", e=2)

        h_t = h_sb[:].rearrange("p (k t) -> p k t", k=16)
        inter_t = inter_sb[:].rearrange("p (e t) -> p e t", e=8)

        def g1_t(e):
            o = (e % 3) * TB
            return g1_sb[:, o:o + TB]

        stb_t = stb_sb[:].rearrange("p (k t) -> p k t", k=8)

        def xbb_s(k):
            o = (k % 2) * TB
            return xbb_sb[:, o:o + TB]

        def sq_s(k):
            o = (k % 2) * TB
            return sq_sb[:, o:o + TB]

        def btsl(s):
            o = (s % 2) * 2048
            return btsl_sb[:, o:o + 2048].rearrange("p (k n) -> p k n", k=16)

        def ctsl(s):
            o = (s % 2) * 1024
            return ctsl_sb[:, o:o + 1024].rearrange("p (k n) -> p k n", k=8)

        acc_t = arF32[:, 0:8192].rearrange("p (k t) -> p k t", k=16)

        def sclv(i):
            o = 8192 + (i % 4) * FD0
            return arF32[:, o:o + FD0]

        mean_b = arF32[:, 10240:10752]
        rstd_b = arF32[:, 10752:11264]

        def thv(k):
            return thA[:, 0:TB]

        progs = {e: [] for e in ("sync", "gpsimd", "vector", "scalar", "tensor")}
        cnt = {n: 0 for n in semnames}

        def emit(eng, fn):
            progs[eng].append(fn)

        def W(eng, sem, val):
            if val > 0:
                emit(eng, lambda e, s=SEM[sem], v=val: e.wait_ge(s, v))

        def INC(sem, k=1):
            cnt[sem] += k
            return cnt[sem]

        def DMA(eng, out, in_, sem):
            v = INC(sem, 16)
            emit(eng, lambda e, o=out, i=in_, s=SEM[sem]: e.dma_start(out=o, in_=i).then_inc(s, 16))
            return v

        DMA("sync", cst[:], cstI, "s_ld")
        DMA("sync", dv[:], Dp, "s_ld")
        DMA("sync", ones_f[:], onesI, "s_ld")
        W("vector", "s_ld", 48)
        emit("vector", lambda e: e.tensor_copy(ones_b[:], ones_f[:]).then_inc(SEM["s_pre"], 1))
        INC("s_pre")
        W("scalar", "s_ld", 48)
        W("gpsimd", "s_ld", 48)

        tiles = []
        for nr in range(NR):
            for c in range(8):
                tiles.append(("g", gpT, gsT, wgS, c, nr))
            for c in range(8):
                tiles.append(("u", upT, usT, wuS, c, nr))
            for c2 in range(2):
                cd = 2 * nr + c2
                for hr in range(4):
                    tiles.append(("d", dpT, dsT, wdS, cd, hr))
        assert len(tiles) == NT

        def tile_srcs(i):
            kind, psrc, ssrc, dst, c, nr = tiles[i]
            pk_ap = psrc[128 * c:128 * (c + 1), FD0 * nr:FD0 * (nr + 1)]
            scl_ap = (ssrc[8 * c:8 * c + 8, FD0 * nr:FD0 * (nr + 1)]
                      .unsqueeze(1).broadcast_to([8, 16, FD0]))
            return pk_ap, scl_ap

        def tile_dsts(i):
            kind, psrc, ssrc, dst, c, nr = tiles[i]
            if kind in ("g", "u"):
                return (dst[2 * c, :, FD0 * nr:FD0 * (nr + 1)],
                        dst[2 * c + 1, :, FD0 * nr:FD0 * (nr + 1)])
            return (dst[:, 2 * c, FD0 * nr:FD0 * (nr + 1)],
                    dst[:, 2 * c + 1, FD0 * nr:FD0 * (nr + 1)])

        def emit_vecA(i):
            pk = pkv(i)
            W("vector", "s_pk", 16 * min(i + 3, NT + 2))
            emit("vector", lambda e, a=u8v(U8_T8, i), b=pk: e.tensor_scalar(a[:], b[:], 8, None, OP.bitwise_and).then_inc(SEM["s_dva"], 1))
            emit("vector", lambda e, a=u8v(U8_HI8, i), b=pk: e.tensor_scalar(a[:], b[:], 4, None, OP.logical_shift_right))
            emit("vector", lambda e, a=u8v(U8_M8L, i), b=pk: e.tensor_scalar(a[:], b[:], 7, None, OP.bitwise_and).then_inc(SEM["s_dva"], 1))
            emit("vector", lambda e, a=u8v(U8_M8H, i), b=u8v(U8_HI8, i): e.tensor_scalar(a[:], b[:], 7, None, OP.bitwise_and).then_inc(SEM["s_dva"], 1))
            emit("vector", lambda e, a=bfv(K_SGL, i), b=u8v(U8_T8, i): e.tensor_scalar(a[:], b[:], -0.25, 1.0, OP.mult, OP.add))

        def emit_scalA(i):
            pk = pkv(i)
            W("scalar", "s_dva", 3 * (i + 1))
            W("scalar", "s_pk", 16 * min(i + 3, NT + 2))
            if i >= 3:
                W("scalar", "s_dvb", i - 2)
                W("scalar", "s_dvc", i - 2)
                W("scalar", "s_wr", 32 * (i - 2))
            for fn_ in (
                lambda e, a=bfv(K_SGH, i), b=pk: e.activation(a[:], b[:], AF.Sign, scale=-1.0, bias=cst[:, C_B1275:C_B1275 + 1]).then_inc(SEM["s_aca"], 1),
                lambda e, a=bfv2(K_R6L, i), b=m8pair(i): e.activation(a[:], b[:], AF.Relu, scale=2.0, bias=cst[:, C_M12:C_M12 + 1]).then_inc(SEM["s_aca"], 1),
                lambda e, a=bfv2(K_MBL, i), b=m8pair(i): e.activation(a[:], b[:], AF.Identity).then_inc(SEM["s_aca"], 1),
            ):
                emit("scalar", fn_)

        def emit_decode_tile(i):
            pk_ap, _ = tile_srcs(i)
            if i >= 6:
                W("sync", "s_aca", 3 * (i - 5))
                W("sync", "s_dva", 3 * (i - 5))
            DMA("sync", pkv(i)[:], pk_ap, "s_pk")

            W("vector", "s_aca", 3 * (i + 1))
            if i >= 2:
                W("vector", "s_poo", i - 1)
                W("vector", "s_wr", 32 * (i - 1))
            emit("vector", lambda e, a=bfv2(K_QL, i), b=bfv2(K_R6L, i), c2=bfv2(K_MBL, i): e.tensor_tensor(a[:], b[:], c2[:], OP.add))
            emit("vector", lambda e, a=bfv2(K_QL, i): e.tensor_scalar(a[:], a[:], -4.0, 0.0, OP.add, OP.max))
            emit("vector", lambda e, a=bfv2(K_R6L, i), b=bfv2(K_MBL, i), c2=bfv2(K_QL, i): e.tensor_tensor(a[:], b[:], c2[:], OP.add))
            emit("vector", lambda e, a=bfv2(K_QL, i), b=bfv2(K_R6L, i), c2=bfv2(K_SGL, i): e.tensor_tensor(a[:], b[:], c2[:], OP.mult).then_inc(SEM["s_dvc"], 1))
            W("vector", "s_scl", 16 * min(i + 3, NT + 2))
            emit("vector", lambda e, a=bfv(K_VL, i), b=bfv(K_QL, i), c2=sclv(i): e.tensor_tensor(a[:], b[:], c2[:], OP.mult).then_inc(SEM["s_dvb"], 1))
            emit("vector", lambda e, a=bfv(K_R6H, i), b=bfv(K_QH, i), c2=sclv(i): e.tensor_tensor(a[:], b[:], c2[:], OP.mult).then_inc(SEM["s_poo"], 1))
            if i + 3 < NT:
                emit_vecA(i + 3)

            emit_scalA(i)
            if i >= 2:
                lo_dst, hi_dst = tile_dsts(i - 2)
                W("scalar", "s_dvb", i - 1)
                W("scalar", "s_poo", i - 1)
                emit("scalar", lambda e, o=lo_dst, s2=bfv(K_VL, i - 2), s=SEM["s_wr"]: e.dma_start(out=o, in_=s2[:]).then_inc(s, 16))
                emit("scalar", lambda e, o=hi_dst, s2=bfv(K_R6H, i - 2), s=SEM["s_wr"]: e.dma_start(out=o, in_=s2[:]).then_inc(s, 16))

            if i + 3 < NT:
                _, scl_ap3 = tile_srcs(i + 3)
                if i >= 1:
                    W("gpsimd", "s_dvb", i)
                    W("gpsimd", "s_poo", i)
                DMA("gpsimd", sclv(i + 3)[:], scl_ap3, "s_scl")
            elif i + 3 in (NT, NT + 1):
                DMA("gpsimd", f32pad[:], gsT[0:1, 0:64], "s_scl")

        def emit_decode_prologue():
            for k in range(min(3, NT)):
                _, scl_ap0 = tile_srcs(k)
                DMA("gpsimd", sclv(k)[:], scl_ap0, "s_scl")
            for k in range(min(3, NT)):
                emit_vecA(k)

        def emit_decode_tail():
            for j in (NT - 2, NT - 1):
                lo_dst, hi_dst = tile_dsts(j)
                W("scalar", "s_dvb", j + 1)
                W("scalar", "s_poo", j + 1)
                emit("scalar", lambda e, o=lo_dst, s2=bfv(K_VL, j), s=SEM["s_wr"]: e.dma_start(out=o, in_=s2[:]).then_inc(s, 16))
                emit("scalar", lambda e, o=hi_dst, s2=bfv(K_R6H, j), s=SEM["s_wr"]: e.dma_start(out=o, in_=s2[:]).then_inc(s, 16))
            for dk in range(4):
                emit("scalar", lambda e, k=dk, s=SEM["s_wr"]: e.dma_start(out=padH[0:1, 16 * k:16 * (k + 1)], in_=f32pad[0:1, 0:16]).then_inc(s, 16))
            cnt["s_wr"] = 32 * NT + 64
            DMA("sync", u8pad[:], gpT[0:1, 0:64], "s_pk")
            DMA("sync", u8pad[:], gpT[0:1, 64:128], "s_pk")

        def unit_wr_need(j):
            return min(32 * TPG * (j // 2 + 2), 32 * NT + 64)

        hist = dict(gelu=[], it=[], ugate=[], uup=[], gpd=[], dnev=[],
                    eta=0, unit=0, tanh_last=0, attn_ev=None, hv1=0, hv2=0)

        def layer_norm(fold05):
            ev0 = cnt["s_ev"]
            W("tensor", "s_ev", ev0)
            W("tensor", "s_pre", 1)
            stat0 = cnt["s_stat"]
            pa0 = cnt["s_pa"]
            for kc in range(16):
                if kc >= 2:
                    W("vector", "s_pa", pa0 + kc - 1)
                emit("vector", lambda e, k=kc: e.tensor_copy(xbb_s(k)[:], acc_t[:, k, :]))
                emit("vector", lambda e, k=kc: e.tensor_tensor(sq_s(k)[:], xbb_s(k)[:], xbb_s(k)[:], OP.mult).then_inc(SEM["s_stat"], 1))
                INC("s_stat")
            W("vector", "s_pa", pa0 + 15)
            emit("vector", lambda e: e.tensor_copy(xbb_s(0)[:], acc_t[:, 0, :]).then_inc(SEM["s_stat"], 1))
            INC("s_stat")
            for kc in range(16):
                W("tensor", "s_stat", min(stat0 + kc + 2, stat0 + 17))
                emit("tensor", lambda e, k=kc: e.matmul(ps_d[0][0:1, :], ones_b[:], xbb_s(k)[:], start=(k == 0), stop=(k == 15)))
                if kc < 15:
                    emit("tensor", lambda e, k=kc: e.matmul(ps_d[1][0:1, :], ones_b[:], sq_s(k)[:], start=(k == 0), stop=False).then_inc(SEM["s_pa"], 1))
                else:
                    emit("tensor", lambda e: e.matmul(ps_d[1][0:1, :], ones_b[:], sq_s(15)[:], start=False, stop=True).then_inc(SEM["s_pa"], 1))
                INC("s_pa")
            W("scalar", "s_pa", cnt["s_pa"])
            emit("scalar", lambda e: e.activation(s_mean[:], ps_d[0][0:1, :], AF.Identity, scale=1.0 / D).then_inc(SEM["s_stat"], 1))
            INC("s_stat")
            emit("scalar", lambda e: e.activation(s_ex2[:], ps_d[1][0:1, :], AF.Identity, scale=1.0 / D).then_inc(SEM["s_stat"], 1))
            sv2 = INC("s_stat")
            W("vector", "s_stat", sv2)
            emit("vector", lambda e: e.tensor_tensor(s_var[:], s_mean[:], s_mean[:], OP.mult))
            emit("vector", lambda e: e.tensor_tensor(s_var[:], s_ex2[:], s_var[:], OP.subtract).then_inc(SEM["s_stat"], 1))
            sv3 = INC("s_stat")
            W("scalar", "s_stat", sv3)
            emit("scalar", lambda e: e.activation(s_ex2[:], s_var[:], AF.Ln, bias=cst[0:1, C_EPS:C_EPS + 1]))
            bc = C_LN05 if fold05 else C_ZERO
            emit("scalar", lambda e, b=bc: e.activation(s_rstd[:], s_ex2[:], AF.Exp, scale=-0.5, bias=cst[0:1, b:b + 1]).then_inc(SEM["s_stat"], 1))
            sv4 = INC("s_stat")
            W("scalar", "s_stat", sv4)
            DMA("scalar", statH[0:1, :], s_mean[:], "s_brd")
            hb = DMA("scalar", statH[1:2, :], s_rstd[:], "s_brd")
            W("scalar", "s_brd", hb)
            DMA("scalar", mean_b[:], statH[0:1, :].unsqueeze(1).broadcast_to([1, 128, TB]), "s_brd")
            DMA("scalar", rstd_b[:], statH[1:2, :].unsqueeze(1).broadcast_to([1, 128, TB]), "s_brd")
            W("vector", "s_brd", cnt["s_brd"])
            for kc in range(16):
                emit("vector", lambda e, k=kc: e.tensor_tensor(thv(k)[:], acc_t[:, k, :], mean_b[:], OP.subtract))
                emit("vector", lambda e, k=kc: e.tensor_tensor(h_t[:, k, :], thv(k)[:], rstd_b[:], OP.mult).then_inc(SEM["s_h"], 1))
            return INC("s_h", 16)

        def emit_attn(p):
            pa0 = cnt["s_pa"]
            pd0 = cnt["s_pd"]
            ge0 = cnt["s_ge"]
            bt_done, ct_done = [], []
            for nsc in range(8):
                if nsc >= 2:
                    W("gpsimd", "s_pa", pa0 + nsc - 1)
                elif p >= 1:
                    W("gpsimd", "s_pa", 40 * (p - 1) + 16 + 7 + nsc)
                else:
                    W("gpsimd", "s_pa", pa0)
                bt_done.append(DMA("gpsimd", btsl(nsc)[:, :, :],
                                   BTp.rearrange("(c p) n -> p c n", p=128)[:, :, 128 * nsc:128 * nsc + 128], "s_bc"))
            for dc in range(16):
                if dc >= 2:
                    W("gpsimd", "s_pd", pd0 + dc - 1)
                elif p >= 1:
                    W("gpsimd", "s_pd", 272 * (p - 1) + 15 + dc)
                else:
                    W("gpsimd", "s_pd", pd0)
                ct_done.append(DMA("gpsimd", ctsl(dc)[:, :, :],
                                   CTp.rearrange("(c p) n -> p c n", p=128)[:, :, 128 * dc:128 * dc + 128], "s_bc"))
            all_bc = DMA("gpsimd", f32pad[:], gsT[0:1, 0:64], "s_bc")

            W("tensor", "s_h", hist["hv1"])
            tanh_ge = []
            for nsc in range(8):
                bank = ps_g[nsc % 3]
                W("tensor", "s_bc", bt_done[nsc + 1] if nsc + 1 < 8 else ct_done[0])
                W("tensor", "s_ge", tanh_ge[nsc - 3] if nsc >= 3 else ge0)
                for kc in range(16):
                    if kc < 15:
                        emit("tensor", lambda e, k=kc, n=nsc, b=bank: e.matmul(b[:], btsl(n)[:, k, :], h_t[:, k, :], start=(k == 0), stop=False))
                    else:
                        emit("tensor", lambda e, n=nsc, b=bank: e.matmul(b[:], btsl(n)[:, 15, :], h_t[:, 15, :], start=False, stop=True).then_inc(SEM["s_pa"], 1))
                pav = INC("s_pa")
                W("scalar", "s_pa", pav)
                emit("scalar", lambda e, n=nsc, b=bank: e.activation(stb_t[:, n, :], b[:], AF.Tanh).then_inc(SEM["s_ge"], 1))
                tanh_ge.append(INC("s_ge"))
            hist["tanh_last"] = cnt["s_ge"]
            ev0 = cnt["s_ev"]
            attn_ev = []
            W("tensor", "s_ge", tanh_ge[7])
            for dc in range(16):
                bank = ps_d[dc % 2]
                W("tensor", "s_bc", ct_done[dc + 1] if dc + 1 < 16 else all_bc)
                W("tensor", "s_ev", attn_ev[dc - 2] if dc >= 2 else ev0)
                for nsc in range(8):
                    if nsc < 7:
                        emit("tensor", lambda e, n=nsc, d2=dc, b=bank: e.matmul(b[:], ctsl(d2)[:, n, :], stb_t[:, n, :], start=(n == 0), stop=False))
                    else:
                        emit("tensor", lambda e, d2=dc, b=bank: e.matmul(b[:], ctsl(d2)[:, 7, :], stb_t[:, 7, :], start=False, stop=True).then_inc(SEM["s_pd"], 1))
                pdv = INC("s_pd")
                W("vector", "s_pd", pdv)
                emit("vector", lambda e, d2=dc, b=bank: e.tensor_tensor(acc_t[:, d2, :], acc_t[:, d2, :], b[:], OP.add))
                emit("vector", lambda e, d2=dc: e.scalar_tensor_tensor(acc_t[:, d2, :], h_t[:, d2, :], dv[:, d2:d2 + 1], acc_t[:, d2, :], OP.mult, OP.add).then_inc(SEM["s_ev"], 1))
                attn_ev.append(INC("s_ev"))
            hist["attn_ev"] = attn_ev

        def emit_pass_prologue(p):
            tsl = slice(TB * p, TB * (p + 1))
            W("sync", "s_st", cnt["s_st"])
            xv = DMA("sync", acc_t[:, :, :], xTs.rearrange("(c p) t -> p c t", p=128)[:, :, tsl], "s_x")
            W("vector", "s_x", xv)
            W("vector", "s_pd", cnt["s_pd"])
            W("vector", "s_pg", cnt["s_pg"])
            W("vector", "s_puu", cnt["s_puu"])
            hist["hv1"] = layer_norm(False)
            emit_attn(p)
            W("vector", "s_ev", hist["attn_ev"][15])
            hist["hv2"] = layer_norm(True)
            W("tensor", "s_h", hist["hv2"])

        def emit_unit_loads(p, j, chase, wd_only=False, skip_wd=False):
            uidx = p * NU + j
            tsl_n = slice(UW * j, UW * (j + 1))
            wslot = j % 2
            if uidx >= 2:
                W("sync", "s_pg", hist["ugate"][uidx - 2])
                W("sync", "s_puu", hist["uup"][uidx - 2])
            e2 = 2 * j
            if e2 >= 10:
                W("sync", "s_pd", hist["gpd"][16 * p + (e2 - 10) // GSZ])
            elif p >= 1:
                g = 15 if e2 < 8 else 14
                W("sync", "s_pd", 272 * (p - 1) + 16 + 16 * (g + 1))
            if chase:
                W("sync", "s_wr", unit_wr_need(j))
            DMA("sync", wg_u(wslot)[:, :, :], wgS[:, :, tsl_n].rearrange("k p n -> p k n"), "s_ul")
            DMA("sync", wu_u(wslot)[:, :, :], wuS[:, :, tsl_n].rearrange("k p n -> p k n"), "s_ul")
            DMA("sync", wd_pair(2 * j)[:, :, :], wdS[:, 2 * j:2 * j + 2, :], "s_ul")

        def emit_ffn_unit(p, j, chase=False, loads=True):
            uidx = hist["unit"]
            wslot = j % 2
            if loads:
                emit_unit_loads(p, j, chase)
            W("tensor", "s_ul", 48 * (uidx + 1))
            for ep in range(EPU):
                E = hist["eta"]
                eloc = E % 128
                gb = ps_g[E % 3]
                ub = ps_u[E % 3]
                geg = max(hist["gelu"][E - 3] if E >= 3 else 0, hist["tanh_last"])
                W("tensor", "s_ge", geg)
                for kk in range(16):
                    if kk < 15:
                        emit("tensor", lambda e, k=kk, pp=ep, b=gb, ws=wslot: e.matmul(b[:], wg_u(ws)[:, k, 128 * pp:128 * pp + 128], h_t[:, k, :], start=(k == 0), stop=False))
                    else:
                        emit("tensor", lambda e, pp=ep, b=gb, ws=wslot: e.matmul(b[:], wg_u(ws)[:, 15, 128 * pp:128 * pp + 128], h_t[:, 15, :], start=False, stop=True).then_inc(SEM["s_pg"], 1))
                pgv = INC("s_pg")
                W("tensor", "s_it", hist["it"][E - 3] if E >= 3 else 0)
                for kk in range(16):
                    if kk < 15:
                        emit("tensor", lambda e, k=kk, pp=ep, b=ub, ws=wslot: e.matmul(b[:], wu_u(ws)[:, k, 128 * pp:128 * pp + 128], h_t[:, k, :], start=(k == 0), stop=False))
                    else:
                        emit("tensor", lambda e, pp=ep, b=ub, ws=wslot: e.matmul(b[:], wu_u(ws)[:, 15, 128 * pp:128 * pp + 128], h_t[:, 15, :], start=False, stop=True).then_inc(SEM["s_puu"], 1))
                puv = INC("s_puu")
                W("scalar", "s_pg", pgv)
                W("scalar", "s_it", hist["it"][E - 3] if E >= 3 else 0)
                emit("scalar", lambda e, b=gb, g=g1_t(E): e.activation(g[:], b[:], AF.Gelu_apprx_tanh).then_inc(SEM["s_ge"], 1))
                hist["gelu"].append(INC("s_ge"))
                W("vector", "s_ge", hist["gelu"][E])
                W("vector", "s_puu", puv)
                if eloc >= GSZ:
                    W("vector", "s_pd", hist["gpd"][-1])
                elif p > 0 and hist["gpd"]:
                    W("vector", "s_pd", hist["gpd"][-1])
                emit("vector", lambda e, b=ub, g=g1_t(E), s2=eloc % GSZ: e.scalar_tensor_tensor(inter_t[:, s2, :], g[:], 0.5, b[:], OP.mult, OP.mult).then_inc(SEM["s_it"], 1))
                hist["it"].append(INC("s_it"))
                hist["eta"] += 1
            hist["ugate"].append(cnt["s_pg"])
            hist["uup"].append(cnt["s_puu"])
            hist["unit"] = uidx + 1

            if j % 4 == 3:
                W("tensor", "s_it", hist["it"][hist["eta"] - 1])
                for hc in range(16):
                    db = ps_d[hc % 2]
                    li = len(hist["dnev"])
                    evg = hist["attn_ev"][14 + (hc % 2)]
                    if li >= 2:
                        evg = max(evg, hist["dnev"][li - 2])
                    W("tensor", "s_ev", evg)
                    for ei in range(GSZ):
                        egl = (hist["eta"] - GSZ) + ei
                        eloc2 = egl % 128
                        esl = (2 * (eloc2 // 2)) % 10 + (eloc2 % 2)
                        if ei < GSZ - 1:
                            emit("tensor", lambda e, s3=esl, h2=hc, s2=ei, b=db: e.matmul(b[:], wd_e(s3)[:, 128 * h2:128 * h2 + 128], inter_t[:, s2, :], start=(s2 == 0), stop=False))
                        else:
                            emit("tensor", lambda e, s3=esl, h2=hc, s2=ei, b=db: e.matmul(b[:], wd_e(s3)[:, 128 * h2:128 * h2 + 128], inter_t[:, s2, :], start=False, stop=True).then_inc(SEM["s_pd"], 1))
                    pdv = INC("s_pd")
                    W("vector", "s_pd", pdv)
                    emit("vector", lambda e, h2=hc, b=db: e.tensor_tensor(acc_t[:, h2, :], acc_t[:, h2, :], b[:], OP.add).then_inc(SEM["s_ev"], 1))
                    hist["dnev"].append(INC("s_ev"))
                hist["gpd"].append(cnt["s_pd"])

        def emit_store(p):
            tsl = slice(TB * p, TB * (p + 1))
            W("sync", "s_ev", hist["dnev"][-1])
            DMA("sync", oT.rearrange("(c p) t -> p c t", p=128)[:, :, tsl], acc_t[:, :, :], "s_st")

        emit_pass_prologue(0)
        emit_decode_prologue()
        ti = 0
        uj = 0
        for nr in range(NR):
            for _ in range(TPG):
                emit_decode_tile(ti)
                ti += 1
            if nr >= LAG_NR:
                tgt = 2 * (nr - LAG_NR) + 2
                while uj < min(tgt, NU):
                    emit_ffn_unit(0, uj, chase=True)
                    uj += 1
        emit_decode_tail()
        while uj < NU:
            emit_ffn_unit(0, uj, chase=True)
            uj += 1
        emit_unit_loads(1, 0, chase=True)
        emit_unit_loads(1, 1, chase=True)
        emit_store(0)
        if DEBUG_DUMPS:
            for nm2 in ("dbg_wgS", "dbg_wuS", "dbg_wdS"):
                if nm2 in dbg:
                    src2 = {"dbg_wgS": wgS, "dbg_wuS": wuS, "dbg_wdS": wdS}[nm2]
                    W("gpsimd", "s_wr", 32 * NT)
                    emit("gpsimd", lambda e, o=dbg[nm2], i=src2, s=SEM["s_st"]: e.dma_start(out=o, in_=i).then_inc(s, 16))
                    INC("s_st", 16)

        for p in range(1, NBLK):
            emit_pass_prologue(p)
            for j in range(NU):
                emit_ffn_unit(p, j, chase=(p == 1), loads=(j >= 2))
            if p + 1 < NBLK:
                emit_unit_loads(p + 1, 0, chase=False)
                emit_unit_loads(p + 1, 1, chase=False)
            emit_store(p)

        W("gpsimd", "s_st", cnt["s_st"])
        W("gpsimd", "s_wr", 32 * NT + 64)

        with nc.Block() as block:
            @block.sync
            def _(eng):
                for fn in progs["sync"]:
                    fn(eng)

            @block.gpsimd
            def _(eng):
                for fn in progs["gpsimd"]:
                    fn(eng)

            @block.vector
            def _(eng):
                for fn in progs["vector"]:
                    fn(eng)

            @block.scalar
            def _(eng):
                for fn in progs["scalar"]:
                    fn(eng)

            @block.tensor
            def _(eng):
                for fn in progs["tensor"]:
                    fn(eng)

    return nc


def kernel(L0_attn_x, L0_attn_B, L0_attn_C, L0_attn_D,
           L0_ffn_gate_W_packed, L0_ffn_gate_scales,
           L0_ffn_up_W_packed, L0_ffn_up_scales,
           L0_ffn_down_W_packed, L0_ffn_down_scales):
    pd_ = _perm(D)
    pn = _perm(NM)

    x = np.ascontiguousarray(np.asarray(L0_attn_x, np.float32).reshape(-1, D).T[pd_])
    BTp = np.ascontiguousarray(np.asarray(L0_attn_B, np.float32).T[pd_])
    CTp = np.ascontiguousarray(np.asarray(L0_attn_C, np.float32).T[:, pd_])
    Dp = np.ascontiguousarray(np.asarray(L0_attn_D, np.float32)[pd_].reshape(16, 128).T)

    def pack_u8(Wp):
        w = np.ascontiguousarray(np.asarray(Wp))
        return w.view(np.uint8).reshape(w.shape[0], w.shape[1], 4)[:, :, 0]

    gpTh = np.ascontiguousarray(pack_u8(L0_ffn_gate_W_packed).T[:, pn])
    upTh = np.ascontiguousarray(pack_u8(L0_ffn_up_W_packed).T[:, pn])
    dpTh = np.ascontiguousarray(pack_u8(L0_ffn_down_W_packed).T[:, pd_])
    gsTh = np.ascontiguousarray(np.asarray(L0_ffn_gate_scales, np.float32).T[:, pn])
    usTh = np.ascontiguousarray(np.asarray(L0_ffn_up_scales, np.float32).T[:, pn])
    dsTh = np.ascontiguousarray(np.asarray(L0_ffn_down_scales, np.float32).T[:, pd_])

    cstv = np.zeros((128, 8), np.float32)
    cstv[:, C_M12] = -12.0
    cstv[:, C_M4] = -4.0
    cstv[:, C_B4] = 4.0
    cstv[:, C_B1275] = 127.5
    cstv[:, C_EPS] = 1e-5
    cstv[:, C_LN05] = LN05
    ones = np.ones((128, 1), np.float32)

    nc = build_program()
    in_maps = []
    for r in range(NCORES):
        in_maps.append(dict(
            xTs=np.ascontiguousarray(x[:, TPC * r:TPC * (r + 1)]),
            gpT=gpTh, upT=upTh, dpT=dpTh, gsT=gsTh, usT=usTh, dsT=dsTh,
            BTp=BTp, CTp=CTp, Dp=Dp, cstI=cstv, onesI=ones,
        ))
    res = run_bass_kernel_spmd(nc, in_maps, core_ids=list(range(NCORES)))
    outT = np.concatenate([r["oT"] for r in res.results], axis=1)
    inv = np.argsort(pd_)
    out = outT[inv].T.reshape(2, 8192, D)
    return np.ascontiguousarray(out, dtype=np.float32)


# version 16
# speedup vs baseline: 1.2847x; 1.0433x over previous
import contextlib
import numpy as np
import concourse.bass as bass
import concourse.mybir as mybir
from concourse.bass_utils import run_bass_kernel_spmd

F32 = mybir.dt.float32
BF16 = mybir.dt.bfloat16
U8 = mybir.dt.uint8
AF = mybir.ActivationFunctionType
OP = mybir.AluOpType

NCORES = 8
D = 2048
NS = 1024
NM = 16384
TPC = 2048
TB = 512
NBLK = 4
FD0 = 512
NR = NM // FD0
TPG = 24
NT = NR * TPG
UW = 256
NU = NM // UW
EPU = 2
GSZ = 8
LAG_NR = 2
LN05 = float(np.log(0.5))

C_M12, C_M4, C_B4, C_B1275, C_EPS, C_LN05, C_ZERO, C_SP = range(8)

DEBUG_DUMPS = False


def _perm(n):
    blk = np.concatenate([np.arange(0, 256, 2), np.arange(1, 256, 2)])
    return (np.arange(n // 256)[:, None] * 256 + blk[None, :]).ravel()


def build_program():
    nc = bass.Bass("TRN2", target_bir_lowering=False)

    xTs = nc.dram_tensor("xTs", [D, TPC], F32, kind="ExternalInput").ap()
    gpT = nc.dram_tensor("gpT", [1024, NM], U8, kind="ExternalInput").ap()
    upT = nc.dram_tensor("upT", [1024, NM], U8, kind="ExternalInput").ap()
    dpT = nc.dram_tensor("dpT", [8192, D], U8, kind="ExternalInput").ap()
    gsT = nc.dram_tensor("gsT", [64, NM], F32, kind="ExternalInput").ap()
    usT = nc.dram_tensor("usT", [64, NM], F32, kind="ExternalInput").ap()
    dsT = nc.dram_tensor("dsT", [512, D], F32, kind="ExternalInput").ap()
    BTp = nc.dram_tensor("BTp", [D, NS], F32, kind="ExternalInput").ap()
    CTp = nc.dram_tensor("CTp", [NS, D], F32, kind="ExternalInput").ap()
    Dp = nc.dram_tensor("Dp", [128, 16], F32, kind="ExternalInput").ap()
    cstI = nc.dram_tensor("cstI", [128, 8], F32, kind="ExternalInput").ap()
    onesI = nc.dram_tensor("onesI", [128, 1], F32, kind="ExternalInput").ap()
    oT = nc.dram_tensor("oT", [D, TPC], F32, kind="ExternalOutput").ap()

    wgS = nc.dram_tensor("wgS", [16, 128, NM], BF16, kind="Internal").ap()
    wuS = nc.dram_tensor("wuS", [16, 128, NM], BF16, kind="Internal").ap()
    wdS = nc.dram_tensor("wdS", [128, 128, D], BF16, kind="Internal").ap()
    statH = nc.dram_tensor("statH", [2, TB], F32, kind="Internal").ap()
    padH = nc.dram_tensor("padH", [1, 64], F32, kind="Internal").ap()

    dbg = {}
    if DEBUG_DUMPS:
        import os
        if os.environ.get("DBG_FULL_SCRATCH"):
            for nm, shp, dt2 in (("dbg_wgS", [16, 128, NM], BF16),
                                 ("dbg_wuS", [16, 128, NM], BF16),
                                 ("dbg_wdS", [128, 128, D], BF16)):
                dbg[nm] = nc.dram_tensor(nm, shp, dt2, kind="ExternalOutput").ap()

    with contextlib.ExitStack() as st_:
        def sbt(name, shape, dt):
            return st_.enter_context(nc.sbuf_tensor(name, shape, dt))

        arU8 = sbt("arU8", [128, 8192], U8)
        arBF = sbt("arBF", [128, 49152], BF16)
        arF32 = sbt("arF32", [128, 11264], F32)
        thA = sbt("thA", [128, 512], F32)
        h_sb = sbt("h_sb", [128, 8192], BF16)
        inter_sb = sbt("inter_sb", [128, 4096], BF16)
        g1_sb = sbt("g1_sb", [128, 3 * TB], BF16)
        stb_sb = sbt("stb_sb", [128, 4096], BF16)
        xbb_sb = sbt("xbb_sb", [128, 1024], BF16)
        sq_sb = sbt("sq_sb", [128, 1024], BF16)
        btsl_sb = sbt("btsl_sb", [128, 4096], BF16)
        ctsl_sb = sbt("ctsl_sb", [128, 2048], BF16)
        cst = sbt("cst", [128, 8], F32)
        dv = sbt("dv", [128, 16], F32)
        ones_f = sbt("ones_f", [128, 1], F32)
        ones_b = sbt("ones_b", [128, 1], BF16)
        u8pad = sbt("u8pad", [1, 64], U8)
        f32pad = sbt("f32pad", [1, 64], F32)
        s_mean = sbt("s_mean", [1, TB], F32)
        s_ex2 = sbt("s_ex2", [1, TB], F32)
        s_var = sbt("s_var", [1, TB], F32)
        s_rstd = s_var

        ps_g = [st_.enter_context(nc.psum_tensor(f"ps_g{i}", [128, TB], F32)) for i in range(3)]
        ps_u = [st_.enter_context(nc.psum_tensor(f"ps_u{i}", [128, TB], F32)) for i in range(3)]
        ps_d = [st_.enter_context(nc.psum_tensor(f"ps_d{i}", [128, TB], F32)) for i in range(2)]

        semnames = ["s_ld", "s_pk", "s_scl", "s_dva", "s_dvb", "s_dvc",
                    "s_aca", "s_poo", "s_wr", "s_x", "s_bc",
                    "s_stat", "s_brd", "s_h", "s_pg", "s_puu", "s_pd",
                    "s_pa", "s_ge", "s_it", "s_ev", "s_ul", "s_st", "s_pre"]
        SEM = {n: st_.enter_context(nc.semaphore(n)) for n in semnames}

        def pkv(i):
            o = (i % 6) * FD0
            return arU8[:, o:o + FD0]

        U8_T8, U8_HI8, U8_M8L, U8_M8H = 3072, 4096, 5120, 5632
        U8_RING = {U8_T8: 2, U8_HI8: 2}

        def u8v(base, i):
            if base in (U8_M8L, U8_M8H):
                o = base + (i % 3) * 1024
            else:
                o = base + (i % U8_RING[base]) * FD0
            return arU8[:, o:o + FD0]

        def m8pair(i):
            o = 5120 + (i % 3) * 1024
            return arU8[:, o:o + 1024]

        (K_MBL, K_MBH, K_SGL, K_SGH, K_R6L, K_R6H) = range(6)
        (K_QL, K_QH, K_VL) = range(6, 9)

        def bfv(kind, i):
            if kind < 6:
                o = (i % 3) * 3072 + kind * FD0
            else:
                o = 9216 + (i % 2) * 1536 + (kind - 6) * FD0
            return arBF[:, o:o + FD0]

        def bfv2(kind, i):
            if kind < 6:
                o = (i % 3) * 3072 + kind * FD0
            else:
                o = 9216 + (i % 2) * 1536 + (kind - 6) * FD0
            return arBF[:, o:o + 2 * FD0]

        BF_WG, BF_WU, BF_WD = 12288, 20480, 28672

        def wg_u(slot):
            o = BF_WG + (slot % 2) * 4096
            return arBF[:, o:o + 4096].rearrange("p (k n) -> p k n", k=16)

        def wu_u(slot):
            o = BF_WU + (slot % 2) * 4096
            return arBF[:, o:o + 4096].rearrange("p (k n) -> p k n", k=16)

        def wd_e(eslot):
            o = BF_WD + (eslot % 10) * 2048
            return arBF[:, o:o + 2048]

        def wd_pair(ebase):
            o = BF_WD + (ebase % 10) * 2048
            return arBF[:, o:o + 4096].rearrange("p (e h) -> p e h", e=2)

        h_t = h_sb[:].rearrange("p (k t) -> p k t", k=16)
        inter_t = inter_sb[:].rearrange("p (e t) -> p e t", e=8)

        def g1_t(e):
            o = (e % 3) * TB
            return g1_sb[:, o:o + TB]

        stb_t = stb_sb[:].rearrange("p (k t) -> p k t", k=8)

        def xbb_s(k):
            o = (k % 2) * TB
            return xbb_sb[:, o:o + TB]

        def sq_s(k):
            o = (k % 2) * TB
            return sq_sb[:, o:o + TB]

        def btsl(s):
            o = (s % 2) * 2048
            return btsl_sb[:, o:o + 2048].rearrange("p (k n) -> p k n", k=16)

        def ctsl(s):
            o = (s % 2) * 1024
            return ctsl_sb[:, o:o + 1024].rearrange("p (k n) -> p k n", k=8)

        acc_t = arF32[:, 0:8192].rearrange("p (k t) -> p k t", k=16)

        def sclv(i):
            o = 8192 + (i % 4) * FD0
            return arF32[:, o:o + FD0]

        mean_b = arF32[:, 10240:10752]
        rstd_b = arF32[:, 10752:11264]

        def thv(k):
            return thA[:, 0:TB]

        progs = {e: [] for e in ("sync", "gpsimd", "vector", "scalar", "tensor")}
        cnt = {n: 0 for n in semnames}

        def emit(eng, fn):
            progs[eng].append(fn)

        def W(eng, sem, val):
            if val > 0:
                emit(eng, lambda e, s=SEM[sem], v=val: e.wait_ge(s, v))

        def INC(sem, k=1):
            cnt[sem] += k
            return cnt[sem]

        def DMA(eng, out, in_, sem):
            v = INC(sem, 16)
            emit(eng, lambda e, o=out, i=in_, s=SEM[sem]: e.dma_start(out=o, in_=i).then_inc(s, 16))
            return v

        DMA("sync", cst[:], cstI, "s_ld")
        DMA("sync", dv[:], Dp, "s_ld")
        DMA("sync", ones_f[:], onesI, "s_ld")
        W("vector", "s_ld", 48)
        emit("vector", lambda e: e.tensor_copy(ones_b[:], ones_f[:]).then_inc(SEM["s_pre"], 1))
        INC("s_pre")
        W("scalar", "s_ld", 48)
        W("gpsimd", "s_ld", 48)

        tiles = []
        for nr in range(NR):
            for c in range(8):
                tiles.append(("g", gpT, gsT, wgS, c, nr))
            for c in range(8):
                tiles.append(("u", upT, usT, wuS, c, nr))
            for c2 in range(2):
                cd = 2 * nr + c2
                for hr in range(4):
                    tiles.append(("d", dpT, dsT, wdS, cd, hr))
        assert len(tiles) == NT

        def tile_srcs(i):
            kind, psrc, ssrc, dst, c, nr = tiles[i]
            pk_ap = psrc[128 * c:128 * (c + 1), FD0 * nr:FD0 * (nr + 1)]
            scl_ap = (ssrc[8 * c:8 * c + 8, FD0 * nr:FD0 * (nr + 1)]
                      .unsqueeze(1).broadcast_to([8, 16, FD0]))
            return pk_ap, scl_ap

        def tile_dsts(i):
            kind, psrc, ssrc, dst, c, nr = tiles[i]
            if kind in ("g", "u"):
                return (dst[2 * c, :, FD0 * nr:FD0 * (nr + 1)],
                        dst[2 * c + 1, :, FD0 * nr:FD0 * (nr + 1)])
            return (dst[:, 2 * c, FD0 * nr:FD0 * (nr + 1)],
                    dst[:, 2 * c + 1, FD0 * nr:FD0 * (nr + 1)])

        def emit_vecA(i):
            pk = pkv(i)
            W("vector", "s_pk", 16 * min(i + 3, NT + 2))
            emit("vector", lambda e, a=u8v(U8_T8, i), b=pk: e.tensor_scalar(a[:], b[:], 8, None, OP.bitwise_and).then_inc(SEM["s_dva"], 1))
            emit("vector", lambda e, a=u8v(U8_HI8, i), b=pk: e.tensor_scalar(a[:], b[:], 4, None, OP.logical_shift_right))
            emit("vector", lambda e, a=u8v(U8_M8L, i), b=pk: e.tensor_scalar(a[:], b[:], 7, None, OP.bitwise_and).then_inc(SEM["s_dva"], 1))
            emit("vector", lambda e, a=u8v(U8_M8H, i), b=u8v(U8_HI8, i): e.tensor_scalar(a[:], b[:], 7, None, OP.bitwise_and).then_inc(SEM["s_dva"], 1))
            emit("vector", lambda e, a=bfv(K_SGL, i), b=u8v(U8_T8, i): e.tensor_scalar(a[:], b[:], -0.25, 1.0, OP.mult, OP.add))

        def emit_scalA(i):
            pk = pkv(i)
            W("scalar", "s_dva", 3 * (i + 1))
            W("scalar", "s_pk", 16 * min(i + 3, NT + 2))
            if i >= 3:
                W("scalar", "s_dvb", i - 2)
                W("scalar", "s_dvc", i - 2)
                W("scalar", "s_wr", 32 * (i - 2))
            for fn_ in (
                lambda e, a=bfv(K_SGH, i), b=pk: e.activation(a[:], b[:], AF.Sign, scale=-1.0, bias=cst[:, C_B1275:C_B1275 + 1]).then_inc(SEM["s_aca"], 1),
                lambda e, a=bfv2(K_R6L, i), b=m8pair(i): e.activation(a[:], b[:], AF.Relu, scale=2.0, bias=cst[:, C_M12:C_M12 + 1]).then_inc(SEM["s_aca"], 1),
                lambda e, a=bfv2(K_MBL, i), b=m8pair(i): e.activation(a[:], b[:], AF.Identity).then_inc(SEM["s_aca"], 1),
            ):
                emit("scalar", fn_)

        def emit_decode_tile(i):
            pk_ap, _ = tile_srcs(i)
            if i >= 6:
                W("sync", "s_aca", 3 * (i - 5))
                W("sync", "s_dva", 3 * (i - 5))
            DMA("sync", pkv(i)[:], pk_ap, "s_pk")

            W("vector", "s_aca", 3 * (i + 1))
            if i >= 2:
                W("vector", "s_poo", i - 1)
                W("vector", "s_wr", 32 * (i - 1))
            emit("vector", lambda e, a=bfv2(K_QL, i), b=bfv2(K_R6L, i), c2=bfv2(K_MBL, i): e.tensor_tensor(a[:], b[:], c2[:], OP.add))
            emit("vector", lambda e, a=bfv2(K_QL, i): e.tensor_scalar(a[:], a[:], -4.0, 0.0, OP.add, OP.max))
            emit("vector", lambda e, a=bfv2(K_R6L, i), b=bfv2(K_MBL, i), c2=bfv2(K_QL, i): e.tensor_tensor(a[:], b[:], c2[:], OP.add))
            emit("vector", lambda e, a=bfv2(K_QL, i), b=bfv2(K_R6L, i), c2=bfv2(K_SGL, i): e.tensor_tensor(a[:], b[:], c2[:], OP.mult).then_inc(SEM["s_dvc"], 1))
            W("vector", "s_scl", 16 * min(i + 3, NT + 2))
            emit("vector", lambda e, a=bfv(K_VL, i), b=bfv(K_QL, i), c2=sclv(i): e.tensor_tensor(a[:], b[:], c2[:], OP.mult).then_inc(SEM["s_dvb"], 1))
            emit("vector", lambda e, a=bfv(K_R6H, i), b=bfv(K_QH, i), c2=sclv(i): e.tensor_tensor(a[:], b[:], c2[:], OP.mult).then_inc(SEM["s_poo"], 1))
            if i + 3 < NT:
                emit_vecA(i + 3)

            emit_scalA(i)
            if i >= 2:
                lo_dst, hi_dst = tile_dsts(i - 2)
                W("gpsimd", "s_dvb", i - 1)
                W("gpsimd", "s_poo", i - 1)
                emit("gpsimd", lambda e, o=lo_dst, s2=bfv(K_VL, i - 2), s=SEM["s_wr"]: e.dma_start(out=o, in_=s2[:]).then_inc(s, 16))
                emit("gpsimd", lambda e, o=hi_dst, s2=bfv(K_R6H, i - 2), s=SEM["s_wr"]: e.dma_start(out=o, in_=s2[:]).then_inc(s, 16))

            if i + 3 < NT:
                _, scl_ap3 = tile_srcs(i + 3)
                if i >= 1:
                    W("gpsimd", "s_dvb", i)
                    W("gpsimd", "s_poo", i)
                DMA("gpsimd", sclv(i + 3)[:], scl_ap3, "s_scl")
            elif i + 3 in (NT, NT + 1):
                DMA("gpsimd", f32pad[:], gsT[0:1, 0:64], "s_scl")

        def emit_decode_prologue():
            for k in range(min(3, NT)):
                _, scl_ap0 = tile_srcs(k)
                DMA("gpsimd", sclv(k)[:], scl_ap0, "s_scl")
            for k in range(min(3, NT)):
                emit_vecA(k)

        def emit_decode_tail():
            for j in (NT - 2, NT - 1):
                lo_dst, hi_dst = tile_dsts(j)
                W("gpsimd", "s_dvb", j + 1)
                W("gpsimd", "s_poo", j + 1)
                emit("gpsimd", lambda e, o=lo_dst, s2=bfv(K_VL, j), s=SEM["s_wr"]: e.dma_start(out=o, in_=s2[:]).then_inc(s, 16))
                emit("gpsimd", lambda e, o=hi_dst, s2=bfv(K_R6H, j), s=SEM["s_wr"]: e.dma_start(out=o, in_=s2[:]).then_inc(s, 16))
            for dk in range(4):
                emit("gpsimd", lambda e, k=dk, s=SEM["s_wr"]: e.dma_start(out=padH[0:1, 16 * k:16 * (k + 1)], in_=f32pad[0:1, 0:16]).then_inc(s, 16))
            cnt["s_wr"] = 32 * NT + 64
            DMA("sync", u8pad[:], gpT[0:1, 0:64], "s_pk")
            DMA("sync", u8pad[:], gpT[0:1, 64:128], "s_pk")

        def unit_wr_need(j):
            return min(32 * TPG * (j // 2 + 2), 32 * NT + 64)

        hist = dict(gelu=[], it=[], ugate=[], uup=[], gpd=[], dnev=[],
                    eta=0, unit=0, tanh_last=0, attn_ev=None, hv1=0, hv2=0)

        def layer_norm(fold05):
            ev0 = cnt["s_ev"]
            W("tensor", "s_ev", ev0)
            W("tensor", "s_pre", 1)
            stat0 = cnt["s_stat"]
            pa0 = cnt["s_pa"]
            for kc in range(16):
                if kc >= 2:
                    W("vector", "s_pa", pa0 + kc - 1)
                emit("vector", lambda e, k=kc: e.tensor_copy(xbb_s(k)[:], acc_t[:, k, :]))
                emit("vector", lambda e, k=kc: e.tensor_tensor(sq_s(k)[:], xbb_s(k)[:], xbb_s(k)[:], OP.mult).then_inc(SEM["s_stat"], 1))
                INC("s_stat")
            W("vector", "s_pa", pa0 + 15)
            emit("vector", lambda e: e.tensor_copy(xbb_s(0)[:], acc_t[:, 0, :]).then_inc(SEM["s_stat"], 1))
            INC("s_stat")
            for kc in range(16):
                W("tensor", "s_stat", min(stat0 + kc + 2, stat0 + 17))
                emit("tensor", lambda e, k=kc: e.matmul(ps_d[0][0:1, :], ones_b[:], xbb_s(k)[:], start=(k == 0), stop=(k == 15)))
                if kc < 15:
                    emit("tensor", lambda e, k=kc: e.matmul(ps_d[1][0:1, :], ones_b[:], sq_s(k)[:], start=(k == 0), stop=False).then_inc(SEM["s_pa"], 1))
                else:
                    emit("tensor", lambda e: e.matmul(ps_d[1][0:1, :], ones_b[:], sq_s(15)[:], start=False, stop=True).then_inc(SEM["s_pa"], 1))
                INC("s_pa")
            W("scalar", "s_pa", cnt["s_pa"])
            emit("scalar", lambda e: e.activation(s_mean[:], ps_d[0][0:1, :], AF.Identity, scale=1.0 / D).then_inc(SEM["s_stat"], 1))
            INC("s_stat")
            emit("scalar", lambda e: e.activation(s_ex2[:], ps_d[1][0:1, :], AF.Identity, scale=1.0 / D).then_inc(SEM["s_stat"], 1))
            sv2 = INC("s_stat")
            W("vector", "s_stat", sv2)
            emit("vector", lambda e: e.tensor_tensor(s_var[:], s_mean[:], s_mean[:], OP.mult))
            emit("vector", lambda e: e.tensor_tensor(s_var[:], s_ex2[:], s_var[:], OP.subtract).then_inc(SEM["s_stat"], 1))
            sv3 = INC("s_stat")
            W("scalar", "s_stat", sv3)
            emit("scalar", lambda e: e.activation(s_ex2[:], s_var[:], AF.Ln, bias=cst[0:1, C_EPS:C_EPS + 1]))
            bc = C_LN05 if fold05 else C_ZERO
            emit("scalar", lambda e, b=bc: e.activation(s_rstd[:], s_ex2[:], AF.Exp, scale=-0.5, bias=cst[0:1, b:b + 1]).then_inc(SEM["s_stat"], 1))
            sv4 = INC("s_stat")
            W("scalar", "s_stat", sv4)
            DMA("scalar", statH[0:1, :], s_mean[:], "s_brd")
            hb = DMA("scalar", statH[1:2, :], s_rstd[:], "s_brd")
            W("scalar", "s_brd", hb)
            DMA("scalar", mean_b[:], statH[0:1, :].unsqueeze(1).broadcast_to([1, 128, TB]), "s_brd")
            DMA("scalar", rstd_b[:], statH[1:2, :].unsqueeze(1).broadcast_to([1, 128, TB]), "s_brd")
            W("vector", "s_brd", cnt["s_brd"])
            for kc in range(16):
                emit("vector", lambda e, k=kc: e.tensor_tensor(thv(k)[:], acc_t[:, k, :], mean_b[:], OP.subtract))
                emit("vector", lambda e, k=kc: e.tensor_tensor(h_t[:, k, :], thv(k)[:], rstd_b[:], OP.mult).then_inc(SEM["s_h"], 1))
            return INC("s_h", 16)

        def emit_attn(p):
            pa0 = cnt["s_pa"]
            pd0 = cnt["s_pd"]
            ge0 = cnt["s_ge"]
            bt_done, ct_done = [], []
            for nsc in range(8):
                if nsc >= 2:
                    W("gpsimd", "s_pa", pa0 + nsc - 1)
                elif p >= 1:
                    W("gpsimd", "s_pa", 40 * (p - 1) + 16 + 7 + nsc)
                else:
                    W("gpsimd", "s_pa", pa0)
                bt_done.append(DMA("gpsimd", btsl(nsc)[:, :, :],
                                   BTp.rearrange("(c p) n -> p c n", p=128)[:, :, 128 * nsc:128 * nsc + 128], "s_bc"))
            for dc in range(16):
                if dc >= 2:
                    W("gpsimd", "s_pd", pd0 + dc - 1)
                elif p >= 1:
                    W("gpsimd", "s_pd", 272 * (p - 1) + 15 + dc)
                else:
                    W("gpsimd", "s_pd", pd0)
                ct_done.append(DMA("gpsimd", ctsl(dc)[:, :, :],
                                   CTp.rearrange("(c p) n -> p c n", p=128)[:, :, 128 * dc:128 * dc + 128], "s_bc"))
            all_bc = DMA("gpsimd", f32pad[:], gsT[0:1, 0:64], "s_bc")

            W("tensor", "s_h", hist["hv1"])
            tanh_ge = []
            for nsc in range(8):
                bank = ps_g[nsc % 3]
                W("tensor", "s_bc", bt_done[nsc + 1] if nsc + 1 < 8 else ct_done[0])
                W("tensor", "s_ge", tanh_ge[nsc - 3] if nsc >= 3 else ge0)
                for kc in range(16):
                    if kc < 15:
                        emit("tensor", lambda e, k=kc, n=nsc, b=bank: e.matmul(b[:], btsl(n)[:, k, :], h_t[:, k, :], start=(k == 0), stop=False))
                    else:
                        emit("tensor", lambda e, n=nsc, b=bank: e.matmul(b[:], btsl(n)[:, 15, :], h_t[:, 15, :], start=False, stop=True).then_inc(SEM["s_pa"], 1))
                pav = INC("s_pa")
                W("scalar", "s_pa", pav)
                emit("scalar", lambda e, n=nsc, b=bank: e.activation(stb_t[:, n, :], b[:], AF.Tanh).then_inc(SEM["s_ge"], 1))
                tanh_ge.append(INC("s_ge"))
            hist["tanh_last"] = cnt["s_ge"]
            ev0 = cnt["s_ev"]
            attn_ev = []
            W("tensor", "s_ge", tanh_ge[7])
            for dc in range(16):
                bank = ps_d[dc % 2]
                W("tensor", "s_bc", ct_done[dc + 1] if dc + 1 < 16 else all_bc)
                W("tensor", "s_ev", attn_ev[dc - 2] if dc >= 2 else ev0)
                for nsc in range(8):
                    if nsc < 7:
                        emit("tensor", lambda e, n=nsc, d2=dc, b=bank: e.matmul(b[:], ctsl(d2)[:, n, :], stb_t[:, n, :], start=(n == 0), stop=False))
                    else:
                        emit("tensor", lambda e, d2=dc, b=bank: e.matmul(b[:], ctsl(d2)[:, 7, :], stb_t[:, 7, :], start=False, stop=True).then_inc(SEM["s_pd"], 1))
                pdv = INC("s_pd")
                W("vector", "s_pd", pdv)
                emit("vector", lambda e, d2=dc, b=bank: e.tensor_tensor(acc_t[:, d2, :], acc_t[:, d2, :], b[:], OP.add))
                emit("vector", lambda e, d2=dc: e.scalar_tensor_tensor(acc_t[:, d2, :], h_t[:, d2, :], dv[:, d2:d2 + 1], acc_t[:, d2, :], OP.mult, OP.add).then_inc(SEM["s_ev"], 1))
                attn_ev.append(INC("s_ev"))
            hist["attn_ev"] = attn_ev

        def emit_pass_prologue(p):
            tsl = slice(TB * p, TB * (p + 1))
            W("sync", "s_st", cnt["s_st"])
            xv = DMA("sync", acc_t[:, :, :], xTs.rearrange("(c p) t -> p c t", p=128)[:, :, tsl], "s_x")
            W("vector", "s_x", xv)
            W("vector", "s_pd", cnt["s_pd"])
            W("vector", "s_pg", cnt["s_pg"])
            W("vector", "s_puu", cnt["s_puu"])
            hist["hv1"] = layer_norm(False)
            emit_attn(p)
            W("vector", "s_ev", hist["attn_ev"][15])
            hist["hv2"] = layer_norm(True)
            W("tensor", "s_h", hist["hv2"])

        def emit_unit_loads(p, j, chase, wd_only=False, skip_wd=False):
            uidx = p * NU + j
            tsl_n = slice(UW * j, UW * (j + 1))
            wslot = j % 2
            if uidx >= 2:
                W("sync", "s_pg", hist["ugate"][uidx - 2])
                W("sync", "s_puu", hist["uup"][uidx - 2])
            e2 = 2 * j
            if e2 >= 10:
                W("sync", "s_pd", hist["gpd"][16 * p + (e2 - 10) // GSZ])
            elif p >= 1:
                g = 15 if e2 < 8 else 14
                W("sync", "s_pd", 272 * (p - 1) + 16 + 16 * (g + 1))
            if chase:
                W("sync", "s_wr", unit_wr_need(j))
            DMA("sync", wg_u(wslot)[:, :, :], wgS[:, :, tsl_n].rearrange("k p n -> p k n"), "s_ul")
            DMA("sync", wu_u(wslot)[:, :, :], wuS[:, :, tsl_n].rearrange("k p n -> p k n"), "s_ul")
            DMA("sync", wd_pair(2 * j)[:, :, :], wdS[:, 2 * j:2 * j + 2, :], "s_ul")

        def emit_ffn_unit(p, j, chase=False, loads=True):
            uidx = hist["unit"]
            wslot = j % 2
            if loads:
                emit_unit_loads(p, j, chase)
            W("tensor", "s_ul", 48 * (uidx + 1))
            for ep in range(EPU):
                E = hist["eta"]
                eloc = E % 128
                gb = ps_g[E % 3]
                ub = ps_u[E % 3]
                geg = max(hist["gelu"][E - 3] if E >= 3 else 0, hist["tanh_last"])
                W("tensor", "s_ge", geg)
                for kk in range(16):
                    if kk < 15:
                        emit("tensor", lambda e, k=kk, pp=ep, b=gb, ws=wslot: e.matmul(b[:], wg_u(ws)[:, k, 128 * pp:128 * pp + 128], h_t[:, k, :], start=(k == 0), stop=False))
                    else:
                        emit("tensor", lambda e, pp=ep, b=gb, ws=wslot: e.matmul(b[:], wg_u(ws)[:, 15, 128 * pp:128 * pp + 128], h_t[:, 15, :], start=False, stop=True).then_inc(SEM["s_pg"], 1))
                pgv = INC("s_pg")
                W("tensor", "s_it", hist["it"][E - 3] if E >= 3 else 0)
                for kk in range(16):
                    if kk < 15:
                        emit("tensor", lambda e, k=kk, pp=ep, b=ub, ws=wslot: e.matmul(b[:], wu_u(ws)[:, k, 128 * pp:128 * pp + 128], h_t[:, k, :], start=(k == 0), stop=False))
                    else:
                        emit("tensor", lambda e, pp=ep, b=ub, ws=wslot: e.matmul(b[:], wu_u(ws)[:, 15, 128 * pp:128 * pp + 128], h_t[:, 15, :], start=False, stop=True).then_inc(SEM["s_puu"], 1))
                puv = INC("s_puu")
                W("scalar", "s_pg", pgv)
                W("scalar", "s_it", hist["it"][E - 3] if E >= 3 else 0)
                emit("scalar", lambda e, b=gb, g=g1_t(E): e.activation(g[:], b[:], AF.Gelu_apprx_tanh).then_inc(SEM["s_ge"], 1))
                hist["gelu"].append(INC("s_ge"))
                W("vector", "s_ge", hist["gelu"][E])
                W("vector", "s_puu", puv)
                if eloc >= GSZ:
                    W("vector", "s_pd", hist["gpd"][-1])
                elif p > 0 and hist["gpd"]:
                    W("vector", "s_pd", hist["gpd"][-1])
                emit("vector", lambda e, b=ub, g=g1_t(E), s2=eloc % GSZ: e.scalar_tensor_tensor(inter_t[:, s2, :], g[:], 0.5, b[:], OP.mult, OP.mult).then_inc(SEM["s_it"], 1))
                hist["it"].append(INC("s_it"))
                hist["eta"] += 1
            hist["ugate"].append(cnt["s_pg"])
            hist["uup"].append(cnt["s_puu"])
            hist["unit"] = uidx + 1

            if j % 4 == 3:
                W("tensor", "s_it", hist["it"][hist["eta"] - 1])
                for hc in range(16):
                    db = ps_d[hc % 2]
                    li = len(hist["dnev"])
                    evg = hist["attn_ev"][14 + (hc % 2)]
                    if li >= 2:
                        evg = max(evg, hist["dnev"][li - 2])
                    W("tensor", "s_ev", evg)
                    for ei in range(GSZ):
                        egl = (hist["eta"] - GSZ) + ei
                        eloc2 = egl % 128
                        esl = (2 * (eloc2 // 2)) % 10 + (eloc2 % 2)
                        if ei < GSZ - 1:
                            emit("tensor", lambda e, s3=esl, h2=hc, s2=ei, b=db: e.matmul(b[:], wd_e(s3)[:, 128 * h2:128 * h2 + 128], inter_t[:, s2, :], start=(s2 == 0), stop=False))
                        else:
                            emit("tensor", lambda e, s3=esl, h2=hc, s2=ei, b=db: e.matmul(b[:], wd_e(s3)[:, 128 * h2:128 * h2 + 128], inter_t[:, s2, :], start=False, stop=True).then_inc(SEM["s_pd"], 1))
                    pdv = INC("s_pd")
                    W("vector", "s_pd", pdv)
                    emit("vector", lambda e, h2=hc, b=db: e.tensor_tensor(acc_t[:, h2, :], acc_t[:, h2, :], b[:], OP.add).then_inc(SEM["s_ev"], 1))
                    hist["dnev"].append(INC("s_ev"))
                hist["gpd"].append(cnt["s_pd"])

        def emit_store(p):
            tsl = slice(TB * p, TB * (p + 1))
            W("sync", "s_ev", hist["dnev"][-1])
            DMA("sync", oT.rearrange("(c p) t -> p c t", p=128)[:, :, tsl], acc_t[:, :, :], "s_st")

        emit_pass_prologue(0)
        emit_decode_prologue()
        ti = 0
        uj = 0
        for nr in range(NR):
            for _ in range(TPG):
                emit_decode_tile(ti)
                ti += 1
            if nr >= LAG_NR:
                tgt = 2 * (nr - LAG_NR) + 2
                while uj < min(tgt, NU):
                    emit_ffn_unit(0, uj, chase=True)
                    uj += 1
        emit_decode_tail()
        while uj < NU:
            emit_ffn_unit(0, uj, chase=True)
            uj += 1
        emit_unit_loads(1, 0, chase=True)
        emit_unit_loads(1, 1, chase=True)
        emit_store(0)
        if DEBUG_DUMPS:
            for nm2 in ("dbg_wgS", "dbg_wuS", "dbg_wdS"):
                if nm2 in dbg:
                    src2 = {"dbg_wgS": wgS, "dbg_wuS": wuS, "dbg_wdS": wdS}[nm2]
                    W("gpsimd", "s_wr", 32 * NT)
                    emit("gpsimd", lambda e, o=dbg[nm2], i=src2, s=SEM["s_st"]: e.dma_start(out=o, in_=i).then_inc(s, 16))
                    INC("s_st", 16)

        for p in range(1, NBLK):
            emit_pass_prologue(p)
            for j in range(NU):
                emit_ffn_unit(p, j, chase=(p == 1), loads=(j >= 2))
            if p + 1 < NBLK:
                emit_unit_loads(p + 1, 0, chase=False)
                emit_unit_loads(p + 1, 1, chase=False)
            emit_store(p)

        W("gpsimd", "s_st", cnt["s_st"])
        W("gpsimd", "s_wr", 32 * NT + 64)

        with nc.Block() as block:
            @block.sync
            def _(eng):
                for fn in progs["sync"]:
                    fn(eng)

            @block.gpsimd
            def _(eng):
                for fn in progs["gpsimd"]:
                    fn(eng)

            @block.vector
            def _(eng):
                for fn in progs["vector"]:
                    fn(eng)

            @block.scalar
            def _(eng):
                for fn in progs["scalar"]:
                    fn(eng)

            @block.tensor
            def _(eng):
                for fn in progs["tensor"]:
                    fn(eng)

    return nc


def kernel(L0_attn_x, L0_attn_B, L0_attn_C, L0_attn_D,
           L0_ffn_gate_W_packed, L0_ffn_gate_scales,
           L0_ffn_up_W_packed, L0_ffn_up_scales,
           L0_ffn_down_W_packed, L0_ffn_down_scales):
    pd_ = _perm(D)
    pn = _perm(NM)

    x = np.ascontiguousarray(np.asarray(L0_attn_x, np.float32).reshape(-1, D).T[pd_])
    BTp = np.ascontiguousarray(np.asarray(L0_attn_B, np.float32).T[pd_])
    CTp = np.ascontiguousarray(np.asarray(L0_attn_C, np.float32).T[:, pd_])
    Dp = np.ascontiguousarray(np.asarray(L0_attn_D, np.float32)[pd_].reshape(16, 128).T)

    def pack_u8(Wp):
        w = np.ascontiguousarray(np.asarray(Wp))
        return w.view(np.uint8).reshape(w.shape[0], w.shape[1], 4)[:, :, 0]

    gpTh = np.ascontiguousarray(pack_u8(L0_ffn_gate_W_packed).T[:, pn])
    upTh = np.ascontiguousarray(pack_u8(L0_ffn_up_W_packed).T[:, pn])
    dpTh = np.ascontiguousarray(pack_u8(L0_ffn_down_W_packed).T[:, pd_])
    gsTh = np.ascontiguousarray(np.asarray(L0_ffn_gate_scales, np.float32).T[:, pn])
    usTh = np.ascontiguousarray(np.asarray(L0_ffn_up_scales, np.float32).T[:, pn])
    dsTh = np.ascontiguousarray(np.asarray(L0_ffn_down_scales, np.float32).T[:, pd_])

    cstv = np.zeros((128, 8), np.float32)
    cstv[:, C_M12] = -12.0
    cstv[:, C_M4] = -4.0
    cstv[:, C_B4] = 4.0
    cstv[:, C_B1275] = 127.5
    cstv[:, C_EPS] = 1e-5
    cstv[:, C_LN05] = LN05
    ones = np.ones((128, 1), np.float32)

    nc = build_program()
    in_maps = []
    for r in range(NCORES):
        in_maps.append(dict(
            xTs=np.ascontiguousarray(x[:, TPC * r:TPC * (r + 1)]),
            gpT=gpTh, upT=upTh, dpT=dpTh, gsT=gsTh, usT=usTh, dsT=dsTh,
            BTp=BTp, CTp=CTp, Dp=Dp, cstI=cstv, onesI=ones,
        ))
    res = run_bass_kernel_spmd(nc, in_maps, core_ids=list(range(NCORES)))
    outT = np.concatenate([r["oT"] for r in res.results], axis=1)
    inv = np.argsort(pd_)
    out = outT[inv].T.reshape(2, 8192, D)
    return np.ascontiguousarray(out, dtype=np.float32)
